# version 07 (final)
# baseline (speedup 1.0000x reference)
import math
from contextlib import ExitStack

import numpy as np
import ml_dtypes

import concourse.bass as bass
import concourse.mybir as mybir
from concourse.bass_utils import run_bass_kernel_spmd
from concourse.bass import get_kernel_semaphore_range

F32 = mybir.dt.float32
BF16 = mybir.dt.bfloat16
AF = mybir.ActivationFunctionType
ALU = mybir.AluOpType
AX = mybir.AxisListType

N_CORES = 8
T = 4096
D = 1024
NT = T // 128
DC = D // 128
TT512 = T // 512

ENGINES = ("tensor", "vector", "scalar", "gpsimd", "sync")


class Prog:

    def __init__(self, nc, ctx):
        self.nc = nc
        self.ctx = ctx
        self.sem = {e: ctx.enter_context(nc.semaphore(f"sem_{e}")) for e in ENGINES}
        self.total = {e: 0 for e in ENGINES}
        self.known = {e: {f: 0 for f in ENGINES} for e in ENGINES}
        self.last_w = {}
        self.last_r = {}
        self.ops = {e: [] for e in ENGINES}
        self.n_ops = 0
        self.const_pool = None
        self.const_slots = {}
        self.const_next = 0

    def cst(self, value):
        v = float(value)
        if v not in self.const_slots:
            i = self.const_next
            self.const_next += 1
            assert i < self.const_pool.shape[1], "const pool exhausted"
            self.const_slots[v] = i
            self.op("vector", lambda ve, i=i: ve.memset(self.const_pool[:, i:i + 1], v),
                    writes=[f"cstpool{i}"])
        i = self.const_slots[v]
        return self.const_pool[:, i:i + 1], f"cstpool{i}"

    def op(self, e, fn, reads=(), writes=(), inc=1, tag="", accum=False):
        is_dma = inc == 16
        dma_kind = ("accum" if accum else "plain") if is_dma else False
        deps = {}

        def add_dep(f, v, dep_dma, kind):
            if f == e and e == "tensor":
                return
            if (f == e and is_dma and dma_kind == "plain" and dep_dma == "plain"
                    and kind == "waw"):
                return
            if v > 0 and deps.get(f, 0) < v:
                deps[f] = v

        reads = list(reads)
        writes = list(writes)
        for k in reads[:]:
            if k.startswith("psb"):
                reads.remove(k)
                writes.append(k)
        for k in reads:
            w = self.last_w.get(k)
            if w is not None:
                add_dep(w[0], w[1], w[2], "raw")
        for k in writes:
            w = self.last_w.get(k)
            if w is not None:
                add_dep(w[0], w[1], w[2], "waw")
            for f, (v, dep_dma) in self.last_r.get(k, {}).items():
                add_dep(f, v, dep_dma, "war")

        waits = []
        for f, v in deps.items():
            if v > self.total[f]:
                raise RuntimeError(f"dep on future op {f}:{v} > {self.total[f]}")
            if self.known[e][f] >= v:
                continue
            waits.append((f, v))
            self.known[e][f] = v

        if inc == 0:
            myval = self.total[e] + 1
        else:
            myval = self.total[e] + inc
            self.total[e] = myval

        for k in reads:
            self.last_r.setdefault(k, {})
            prev = self.last_r[k].get(e, (0, dma_kind))[0]
            if myval > prev:
                self.last_r[k][e] = (myval, dma_kind)
        for k in writes:
            self.last_w[k] = (e, myval, dma_kind)
            self.last_r[k] = {}

        self.ops[e].append((fn, waits, inc, tag))
        self.n_ops += 1

    def barrier(self):
        snapshot = dict(self.total)
        for e in ENGINES:
            waits = []
            for f in ENGINES:
                v = snapshot[f]
                if v > 0 and self.known[e][f] < v:
                    waits.append((f, v))
                    self.known[e][f] = v
            self.total[e] += 1
            self.ops[e].append((lambda eng: eng.nop(), waits, 1, "barrier"))
            self.n_ops += 1

    def finalize(self, final_waits=True):
        nc = self.nc
        sem = self.sem
        with nc.Block("semclear") as b0:
            @b0.gpsimd
            def _(g):
                ksr = get_kernel_semaphore_range()
                start = ksr.start
                off = 2
                if nc._bir_kernel_barrier_sem is not None:
                    off += 1
                off += len(nc._monotonic_sems)
                rng = range(start + off, ksr.stop)
                g.dma_reset(rng)
                g.sem_clear(rng)
        block = self.ctx.enter_context(nc.Block())
        for e in ENGINES:
            ops_e = self.ops[e]

            def body(eng, ops_e=ops_e, e=e):
                for fn, waits, inc, tag in ops_e:
                    for f, v in waits:
                        eng.wait_ge(sem[f], v)
                    inst = fn(eng)
                    if inc:
                        inst.then_inc(sem[e], inc)

            getattr(block, e)(body)
        if final_waits:
            @block.sync
            def _(sync):
                for f in ENGINES:
                    if f != "sync" and self.total[f] > 0:
                        sync.wait_ge(sem[f], self.total[f])



class Arena:
    def __init__(self, nc, ctx, p=None):
        self.nc = nc
        self.stack = [ctx]
        self.tensors = {}
        self.p = p

    def sb(self, name, shape, dtype):
        t = self.stack[-1].enter_context(self.nc.sbuf_tensor(name, shape, dtype))
        self.tensors[name] = t
        return t

    def ps(self, name, shape, dtype=F32):
        t = self.stack[-1].enter_context(self.nc.psum_tensor(name, shape, dtype))
        self.tensors[name] = t
        return t

    def scope(self):
        arena = self

        class _S:
            def __enter__(self_s):
                arena.stack.append(ExitStack())

            def __exit__(self_s, *exc):
                if arena.p is not None:
                    arena.p.barrier()
                arena.stack.pop().close()
                return False

        return _S()



def rope_tables_L0(S=T, head_dim=64, theta=10000.0):
    inv_freq = 1.0 / theta ** (np.arange(0, head_dim, 2, dtype=np.float64) / head_dim)
    freqs = np.arange(S, dtype=np.float64)[:, None] * inv_freq[None, :]
    emb = np.concatenate([freqs, freqs], axis=-1)
    return (np.cos(emb)[:, 0::2].astype(np.float32),
            np.sin(emb)[:, 0::2].astype(np.float32))


def rope_tables_L2(S=T, head_dim=64):
    pos = np.arange(S, dtype=np.float64)
    inv = np.power(10000.0, -2.0 * np.arange(head_dim // 2, dtype=np.float64) / head_dim)
    ang = pos[:, None] * inv[None, :]
    cos_pos = np.repeat(np.cos(ang), 2, axis=-1)
    sin_pos = np.repeat(np.sin(ang), 2, axis=-1)
    return cos_pos.astype(np.float32), sin_pos.astype(np.float32)



def ln_pass(p, a, src, dst16, pref, gamma=None, beta=None, eps=1e-5,
            dst32=None):
    nc = p.nc
    nb = 2
    xs = [a.sb(f"{pref}_x{i}", [128, D], F32) for i in range(nb)]
    sq = a.sb(f"{pref}_sq", [128, D], F32)
    st = a.sb(f"{pref}_st", [128, 6], F32)
    o16 = [a.sb(f"{pref}_o{i}", [128, D], BF16) for i in range(nb)]
    o32 = [a.sb(f"{pref}_p{i}", [128, D], F32) for i in range(nb)] if dst32 is not None else None

    for i in range(NT):
        b = i % nb
        x_t, o_t = xs[b], o16[b]
        kx, ko = f"{pref}_x{b}", f"{pref}_o{b}"
        p.op("sync", lambda s, x_t=x_t, i=i: s.dma_start(out=x_t[:], in_=src[i * 128:(i + 1) * 128, :]),
             reads=[src.name], writes=[kx], inc=16)
        p.op("vector", lambda v, x_t=x_t: v.reduce_sum(st[:, 0:1], x_t[:], axis=AX.X),
             reads=[kx], writes=[f"{pref}_st0"])
        p.op("scalar", lambda s, x_t=x_t: s.activation(sq[:], x_t[:], AF.Square, accum_out=st[:, 1:2]),
             reads=[kx], writes=[f"{pref}_sq", f"{pref}_st1"])
        p.op("vector", lambda v: v.tensor_scalar(st[:, 2:3], st[:, 0:1], 1.0 / D, None, ALU.mult),
             reads=[f"{pref}_st0"], writes=[f"{pref}_st2"])
        p.op("vector", lambda v: v.tensor_mul(st[:, 3:4], st[:, 2:3], st[:, 2:3]),
             reads=[f"{pref}_st2"], writes=[f"{pref}_st3"])
        p.op("vector", lambda v: v.scalar_tensor_tensor(st[:, 4:5], st[:, 1:2], 1.0 / D, st[:, 3:4], ALU.mult, ALU.subtract),
             reads=[f"{pref}_st1", f"{pref}_st3"], writes=[f"{pref}_st4"])
        eps_c, eps_k = p.cst(float(eps))
        mh_c, mh_k = p.cst(-0.5)
        p.op("scalar", lambda s: s.activation(st[:, 5:6], st[:, 4:5], AF.Ln, bias=eps_c),
             reads=[f"{pref}_st4", eps_k], writes=[f"{pref}_st5"])
        p.op("scalar", lambda s: s.activation(st[:, 4:5], st[:, 5:6], AF.Exp, scale=mh_c),
             reads=[f"{pref}_st5", mh_k], writes=[f"{pref}_st4"])
        p.op("vector", lambda v, x_t=x_t, o_t=o_t: v.tensor_scalar(o_t[:], x_t[:], st[:, 2:3], st[:, 4:5], ALU.subtract, ALU.mult),
             reads=[kx, f"{pref}_st2", f"{pref}_st4"], writes=[ko])
        p.op("sync", lambda s, o_t=o_t, i=i: s.dma_start(out=dst16[i * 128:(i + 1) * 128, :], in_=o_t[:]),
             reads=[ko], writes=[dst16.name], inc=16)
        if dst32 is not None:
            o3 = o32[b]
            ko3 = f"{pref}_p{b}"
            p.op("vector", lambda v, x_t=x_t, o3=o3: v.tensor_scalar(o3[:], x_t[:], st[:, 2:3], st[:, 4:5], ALU.subtract, ALU.mult),
                 reads=[kx, f"{pref}_st2", f"{pref}_st4"], writes=[ko3])
            p.op("sync", lambda s, o3=o3, i=i: s.dma_start(out=dst32[i * 128:(i + 1) * 128, :], in_=o3[:]),
                 reads=[ko3], writes=[dst32.name], inc=16)


def load_weight_bf16(p, a, w_dram, name, transpose_pre=None):
    K, N = w_dram.shape
    KC = K // 128
    ws = a.sb(name, [128, KC * N], BF16)
    wv = w_dram.rearrange("(kc p) n -> p kc n", p=128)
    p.op("gpsimd", lambda g: g.dma_start(out=ws[:].rearrange("p (kc n) -> p kc n", kc=KC), in_=wv),
         reads=[w_dram.name], writes=[name], inc=16)
    return ws


def load_row(p, a, v_dram, name):
    N = v_dram.shape[-1]
    r = a.sb(name, [1, N], F32)
    p.op("sync", lambda s: s.dma_start(out=r[:], in_=v_dram.reshape([1, N])[:]),
         reads=[v_dram.name], writes=[name], inc=16)
    return r


def row_to_col(p, a, row_sb, row_key, name, n, ident, psbank, pskey):
    col = a.sb(name, [n, 1], F32)
    p.op("tensor", lambda t: t.transpose(psbank[0:n, 0:1], row_sb[0:1, 0:n], ident[0:1, 0:1]),
         reads=[row_key], writes=[pskey])
    p.op("scalar", lambda s: s.copy(col[:], psbank[0:n, 0:1]),
         reads=[pskey], writes=[name])
    return col


def fm_loads(p, chunks_sb, keys, src16, t0=0, tw=T, kc_list=None):
    if kc_list is None:
        kc_list = range(len(chunks_sb))
    for kc in kc_list:
        cs = chunks_sb[kc]
        p.op("sync", lambda s, cs=cs, kc=kc: s.dma_start(
            out=cs[:, 0:tw], in_=src16[t0:t0 + tw, kc * 128:(kc + 1) * 128], transpose=True),
            reads=[src16.name], writes=[keys[kc]], inc=16)


def proj_fm(p, rhs_chunks, rhs_keys, w_sb, w_key, wN, bias_row, bias_key,
            ones_row, psbanks, pskeys, epilogue, t0=0, tw=T, n0=0, nw=None,
            kc_list=None, chunks_t0=None):
    if nw is None:
        nw = wN
    if chunks_t0 is None:
        chunks_t0 = t0
    if kc_list is None:
        kc_list = range(len(rhs_chunks))
    kc_list = list(kc_list)
    nb = len(psbanks)
    n_chunks = [(n0 + i * 128, min(128, n0 + nw - (n0 + i * 128))) for i in range((nw + 127) // 128)]
    t_tiles = [(t0 + i * 512, min(512, t0 + tw - (t0 + i * 512))) for i in range((tw + 511) // 512)]
    bi = 0
    for nc_abs, nlen in n_chunks:
        for tt_abs, tlen in t_tiles:
            ps, pskey = psbanks[bi % nb], pskeys[bi % nb]
            bi += 1
            first = True
            for j, kc in enumerate(kc_list):
                rhs_ap_fn = (lambda t, kc=kc, tt_abs=tt_abs, tlen=tlen, ps=ps, nc_abs=nc_abs, nlen=nlen, first=first:
                             t.matmul(ps[0:nlen, 0:tlen],
                                      w_sb[:, kc * wN + nc_abs: kc * wN + nc_abs + nlen],
                                      rhs_chunks[kc][:, tt_abs - chunks_t0: tt_abs - chunks_t0 + tlen],
                                      start=first, stop=False))
                p.op("tensor", rhs_ap_fn,
                     reads=[w_key, rhs_keys[kc]] + ([pskey] if first else []),
                     writes=([pskey] if first else []), inc=0)
                first = False
            if bias_row is not None:
                p.op("tensor", lambda t, nc_abs=nc_abs, nlen=nlen, tt_abs=tt_abs, tlen=tlen, ps=ps:
                     t.matmul(ps[0:nlen, 0:tlen],
                              bias_row[0:1, nc_abs:nc_abs + nlen],
                              ones_row[0:1, 0:tlen],
                              start=False, stop=True),
                     reads=[bias_key], writes=[pskey], inc=1)
            else:
                raise AssertionError("proj_fm requires bias_row (use a zeros row)")
            epilogue(p, ps, pskey, nc_abs, tt_abs, tlen, nlen)


def proj_tok(p, rhs_chunks, rhs_keys, w_sb, w_key, wN, bias_row, bias_key,
             ones_col, ones_col_key, psbanks, pskeys, epilogue, t0=0, tw=T,
             kc_list=None, chunks_t0=None):
    if chunks_t0 is None:
        chunks_t0 = t0
    if kc_list is None:
        kc_list = range(len(rhs_chunks))
    kc_list = list(kc_list)
    nb = len(psbanks)
    t_tiles = [(t0 + i * 128, 128) for i in range(tw // 128)]
    n_slices = [(i * 512, min(512, wN - i * 512)) for i in range((wN + 511) // 512)]
    bi = 0
    for tt_abs, tlen in t_tiles:
        for ns_abs, nlen in n_slices:
            ps, pskey = psbanks[bi % nb], pskeys[bi % nb]
            bi += 1
            first = True
            for kc in kc_list:
                p.op("tensor", lambda t, kc=kc, tt_abs=tt_abs, ns_abs=ns_abs, nlen=nlen, ps=ps, first=first:
                     t.matmul(ps[0:128, 0:nlen],
                              rhs_chunks[kc][:, tt_abs - chunks_t0: tt_abs - chunks_t0 + 128],
                              w_sb[:, kc * wN + ns_abs: kc * wN + ns_abs + nlen],
                              start=first, stop=False),
                     reads=[w_key, rhs_keys[kc]] + ([pskey] if first else []),
                     writes=([pskey] if first else []), inc=0)
                first = False
            p.op("tensor", lambda t, ns_abs=ns_abs, nlen=nlen, ps=ps:
                 t.matmul(ps[0:128, 0:nlen], ones_col[0:1, 0:128],
                          bias_row[0:1, ns_abs:ns_abs + nlen], start=False, stop=True),
                 reads=[bias_key, ones_col_key], writes=[pskey], inc=1)
            epilogue(p, ps, pskey, tt_abs, ns_abs, nlen)



def accum_epilogue_tok(p, scratch, x_res):
    def epi(p_, ps, pskey, tt_abs, ns_abs, nlen):
        i = scratch["bounce_ctr"][0] % 2
        scratch["bounce_ctr"][0] += 1
        b = scratch["bounce_ring"][i]
        bk = f"bnc{i}"
        p_.op("scalar", lambda s: s.copy(b[0:128, 0:nlen], ps[0:128, 0:nlen]),
              reads=[pskey], writes=[bk])
        p_.op("gpsimd", lambda g: g.dma_start(out=x_res[tt_abs:tt_abs + 128, ns_abs:ns_abs + nlen],
                                              in_=b[0:128, 0:nlen], accum_op=ALU.add),
              reads=[bk], writes=[x_res.name], inc=16, accum=True)
    return epi


L1_STAGE = 99
L1_SUB = 99

def build_L1_attn(p, a, io, x_res, scratch):
    nc = p.nc
    xn16 = scratch["xn16"]
    ident = scratch["ident_sb"]
    psb, psk = scratch["psb"], scratch["psk"]
    x1_32 = scratch["dram_f32_DT"]
    h_dram = scratch["dram_bf16_DT"]

    with a.scope():
        ln_pass(p, a, x_res, xn16, "l1ln")
    if L1_STAGE < 2:
        return

    with a.scope():
        ones = a.sb("l1_ones", [1, 512], F32)
        p.op("vector", lambda v: v.memset(ones[:], 1.0), writes=["l1_ones"])
        a_param = load_row(p, a, io["L1_attn_a_param"], "l1_ap")
        if L1_SUB < 2:
            return
        sp_t1 = a.sb("l1_sp1", [1, D], F32)
        sp_t2 = a.sb("l1_sp2", [1, D], F32)
        sp_row = a.sb("l1_sprow", [1, D], F32)
        p.op("scalar", lambda s: s.activation(sp_t1[:], a_param[:], AF.Abs),
             reads=["l1_ap"], writes=["l1_sp1"])
        m1_c, m1_k = p.cst(-1.0)
        p.op("scalar", lambda s: s.activation(sp_t1[:], sp_t1[:], AF.Exp, scale=m1_c[0:1, 0:1]),
             reads=["l1_sp1", m1_k], writes=["l1_sp1"])
        p.op("scalar", lambda s: s.activation(sp_t1[:], sp_t1[:], AF.Ln, bias=1.0),
             reads=["l1_sp1"], writes=["l1_sp1"])
        p.op("vector", lambda v: v.tensor_scalar_max(sp_t2[:], a_param[:], 0.0),
             reads=["l1_ap"], writes=["l1_sp2"])
        p.op("vector", lambda v: v.tensor_add(sp_row[:], sp_t1[:], sp_t2[:]),
             reads=["l1_sp1", "l1_sp2"], writes=["l1_sprow"])
        sp_col = a.sb("l1_spcol", [128, DC], F32)
        for kc in range(DC):
            p.op("tensor", lambda t, kc=kc: t.transpose(psb[0][0:128, 0:1], sp_row[0:1, kc * 128:(kc + 1) * 128], ident[0:1, 0:1]),
                 reads=["l1_sprow"], writes=[psk[0]])
            p.op("scalar", lambda s, kc=kc: s.copy(sp_col[:, kc:kc + 1], psb[0][0:128, 0:1]),
                 reads=[psk[0]], writes=["l1_spcol"])

        x1_16 = [a.sb(f"l1_x1_{kc}", [128, T], BF16) for kc in range(DC)]
        x1k = [f"l1_x1_{kc}" for kc in range(DC)]

        with a.scope():
            w_in = load_weight_bf16(p, a, io["L1_attn_W_in"], "l1_win")
            b_in = load_row(p, a, io["L1_attn_b_in"], "l1_bin")
            if L1_SUB < 3:
                return
            xf = [a.sb(f"l1_xf{kc}", [128, T], BF16) for kc in range(DC)]
            xfk = [f"l1_xf{kc}" for kc in range(DC)]
            fm_loads(p, xf, xfk, xn16)
            if L1_SUB < 4:
                return

            def epi_x1(p, ps, pskey, nc_abs, tt_abs, tlen, nlen):
                kc = nc_abs // 128
                p.op("vector", lambda v: v.tensor_copy(x1_16[kc][:, tt_abs:tt_abs + tlen], ps[0:128, 0:tlen]),
                     reads=[pskey], writes=[x1k[kc]])
                i = scratch["bounce_ctr"][0] % 2
                scratch["bounce_ctr"][0] += 1
                b = scratch["bounce_ring"][i]
                bk = f"bnc{i}"
                p.op("scalar", lambda s, b=b: s.copy(b[:, 0:tlen], ps[0:128, 0:tlen]),
                     reads=[pskey], writes=[bk])
                p.op("sync", lambda s, b=b: s.dma_start(out=x1_32[nc_abs:nc_abs + nlen, tt_abs:tt_abs + tlen], in_=b[0:nlen, 0:tlen]),
                     reads=[bk], writes=["l1x1d"], inc=16)

            if L1_SUB == 4:
                proj_fm(p, xf, xfk, w_in, "l1_win", D, b_in, "l1_bin", ones, psb, psk, epi_x1,
                        n0=0, nw=128, t0=0, tw=512)
            elif L1_SUB == 5:
                proj_fm(p, xf, xfk, w_in, "l1_win", D, b_in, "l1_bin", ones, psb, psk, epi_x1,
                        n0=0, nw=128)
            elif L1_SUB == 6:
                proj_fm(p, xf, xfk, w_in, "l1_win", D, b_in, "l1_bin", ones, psb, psk, epi_x1,
                        n0=0, nw=256)
            elif L1_SUB == 7:
                proj_fm(p, xf, xfk, w_in, "l1_win", D, b_in, "l1_bin", ones, psb, psk, epi_x1,
                        n0=0, nw=512)
            elif L1_SUB == 8:
                proj_fm(p, xf, xfk, w_in, "l1_win", D, b_in, "l1_bin", ones, psb, psk, epi_x1,
                        n0=512, nw=512)
            elif L1_SUB == 9:
                proj_fm(p, xf, xfk, w_in, "l1_win", D, b_in, "l1_bin", ones, psb, psk, epi_x1,
                        n0=0, nw=768)
            else:
                proj_fm(p, xf, xfk, w_in, "l1_win", D, b_in, "l1_bin", ones, psb, psk, epi_x1)
        if L1_STAGE < 3:
            return

        with a.scope():
            w_gx = load_weight_bf16(p, a, io["L1_attn_W_gx"], "l1_wgx")
            w_ga = load_weight_bf16(p, a, io["L1_attn_W_ga"], "l1_wga")
            b_gx = load_row(p, a, io["L1_attn_b_gx"], "l1_bgx")
            b_ga = load_row(p, a, io["L1_attn_b_ga"], "l1_bga")
            ga_f = a.sb("l1_gaf", [128, T], F32)
            gx_f = a.sb("l1_gxf", [128, T], F32)
            x1f = a.sb("l1_x1f", [128, T], F32)
            uf = a.sb("l1_uf", [128, T], F32)
            h16t = a.sb("l1_h16t", [128, T], BF16)

            for kc in range(DC):
                def epi_ga(p, ps, pskey, nc_abs, tt_abs, tlen, nlen):
                    p.op("scalar", lambda s: s.activation(ga_f[:, tt_abs:tt_abs + tlen], ps[0:128, 0:tlen], AF.Sigmoid),
                         reads=[pskey], writes=["l1_gaf"])
                proj_fm(p, x1_16, x1k, w_ga, "l1_wga", D, b_ga, "l1_bga", ones, psb, psk,
                        epi_ga, n0=kc * 128, nw=128)

                def epi_gx(p, ps, pskey, nc_abs, tt_abs, tlen, nlen):
                    p.op("scalar", lambda s: s.activation(gx_f[:, tt_abs:tt_abs + tlen], ps[0:128, 0:tlen], AF.Sigmoid),
                         reads=[pskey], writes=["l1_gxf"])
                proj_fm(p, x1_16, x1k, w_gx, "l1_wgx", D, b_gx, "l1_bgx", ones, psb, psk,
                        epi_gx, n0=kc * 128, nw=128)

                p.op("sync", lambda s, kc=kc: s.dma_start(out=x1f[:], in_=x1_32[kc * 128:(kc + 1) * 128, :]),
                     reads=["l1x1d"], writes=["l1_x1f"], inc=16)
                p.op("vector", lambda v, kc=kc: v.tensor_scalar(ga_f[:], ga_f[:], sp_col[:, kc:kc + 1], -8.0, ALU.mult, ALU.mult),
                     reads=["l1_gaf", "l1_spcol"], writes=["l1_gaf"])
                p.op("scalar", lambda s: s.activation(ga_f[:], ga_f[:], AF.Exp),
                     reads=["l1_gaf"], writes=["l1_gaf"])
                p.op("vector", lambda v: v.tensor_mul(uf[:], ga_f[:], ga_f[:]),
                     reads=["l1_gaf"], writes=["l1_uf"])
                p.op("vector", lambda v: v.tensor_scalar(uf[:], uf[:], -1.0, 1.0, ALU.mult, ALU.add),
                     reads=["l1_uf"], writes=["l1_uf"])
                p.op("scalar", lambda s: s.activation(uf[:], uf[:], AF.Sqrt),
                     reads=["l1_uf"], writes=["l1_uf"])
                p.op("vector", lambda v: v.tensor_mul(uf[:], uf[:], gx_f[:]),
                     reads=["l1_uf", "l1_gxf"], writes=["l1_uf"])
                p.op("vector", lambda v: v.tensor_mul(uf[:], uf[:], x1f[:]),
                     reads=["l1_uf", "l1_x1f"], writes=["l1_uf"])
                p.op("vector", lambda v: v.tensor_tensor_scan(h16t[:], ga_f[:], uf[:], 0.0, ALU.mult, ALU.add),
                     reads=["l1_gaf", "l1_uf"], writes=["l1_h16t"])
                p.op("sync", lambda s, kc=kc: s.dma_start(out=h_dram[kc * 128:(kc + 1) * 128, :], in_=h16t[:]),
                     reads=["l1_h16t"], writes=["l1hd"], inc=16)

        if L1_STAGE < 4:
            return
        with a.scope():
            w_out = load_weight_bf16(p, a, io["L1_attn_W_out"], "l1_wout")
            b_out = load_row(p, a, io["L1_attn_b_out"], "l1_bout")
            h16 = [a.sb(f"l1_h{kc}", [128, T], BF16) for kc in range(DC)]
            hk = [f"l1_h{kc}" for kc in range(DC)]
            for kc in range(DC):
                p.op("sync", lambda s, kc=kc: s.dma_start(out=h16[kc][:], in_=h_dram[kc * 128:(kc + 1) * 128, :]),
                     reads=["l1hd"], writes=[hk[kc]], inc=16)
            proj_tok(p, h16, hk, w_out, "l1_wout", D, b_out, "l1_bout", ones, "l1_ones",
                     psb, psk, accum_epilogue_tok(p, scratch, x_res))


def build_L1_ffn(p, a, io, x_res, scratch):
    nc = p.nc
    xn16 = scratch["xn16"]
    psb, psk = scratch["psb"], scratch["psk"]
    with a.scope():
        ln_pass(p, a, x_res, xn16, "l1fln")

    with a.scope():
        cb16d = scratch["dram_bf16_DT"]
        cb = io["L1_ffn_CB"]
        p.op("gpsimd", lambda g: g.dma_start(out=cb16d[0:D, 0:D], in_=cb[:]),
             reads=[cb.name], writes=["cbstage"], inc=16)
        cbT = a.sb("l1f_cbt", [128, DC * D], BF16)
        for kc in range(DC):
            p.op("sync", lambda s, kc=kc: s.dma_start(
                out=cbT[:, kc * D:(kc + 1) * D], in_=cb16d[0:D, kc * 128:(kc + 1) * 128], transpose=True),
                reads=["cbstage"], writes=["l1f_cbt"], inc=16)

        scb = load_row(p, a, io["L1_ffn_SCB"], "l1f_scb")
        bias = load_row(p, a, io["L1_ffn_bias"], "l1f_bias")
        scb_sc = a.sb("l1f_scbs", [1, D], F32)
        p.op("vector", lambda v: v.tensor_scalar(scb_sc[:], scb[:], 1.0 / 127.0, None, ALU.mult),
             reads=["l1f_scb"], writes=["l1f_scbs"])

        xf = [a.sb(f"l1f_xf{kc}", [128, T], BF16) for kc in range(DC)]
        xfk = [f"l1f_xf{kc}" for kc in range(DC)]
        fm_loads(p, xf, xfk, xn16)
        ones = a.sb("l1f_ones", [1, 512], F32)
        p.op("vector", lambda v: v.memset(ones[:], 1.0), writes=["l1f_ones"])

        scbb = a.sb("l1f_scbb", [128, D], F32)
        for half in range(2):
            p.op("tensor", lambda t, half=half: t.matmul(psb[2][0:128, 0:512], ones[0:1, 0:128],
                                                         scb_sc[0:1, half * 512:(half + 1) * 512], start=True, stop=True),
                 reads=["l1f_scbs", "l1f_ones"], writes=[psk[2]], inc=1)
            p.op("vector", lambda v, half=half: v.tensor_copy(scbb[:, half * 512:(half + 1) * 512], psb[2][0:128, 0:512]),
                 reads=[psk[2]], writes=["l1f_scbb"])
        for kc in range(DC):
            p.op("vector", lambda v, kc=kc: v.tensor_mul(cbT[:, kc * D:(kc + 1) * D],
                                                         cbT[:, kc * D:(kc + 1) * D], scbb[:]),
                 reads=["l1f_cbt", "l1f_scbb"], writes=["l1f_cbt"])

        proj_tok(p, xf, xfk, cbT, "l1f_cbt", D, bias, "l1f_bias", ones, "l1f_ones",
                 psb, psk, accum_epilogue_tok(p, scratch, x_res))





L0_WEIGHTS = [
    ("L0_attn_Wq", (D, D)), ("L0_attn_Wk", (D, 256)), ("L0_attn_Wv", (D, 256)),
    ("L0_attn_Wo", (D, D)),
    ("L0_ffn_rW", (D, 8)), ("L0_ffn_rb", (8,)),
    ("L0_ffn_W1", (8, D, 4096)), ("L0_ffn_b1", (8, 4096)),
    ("L0_ffn_W2", (8, 4096, D)), ("L0_ffn_b2", (8, D)),
    ("L0_ffn_sW1", (2, D, 4096)), ("L0_ffn_sb1", (2, 4096)),
    ("L0_ffn_sW2", (2, 4096, D)), ("L0_ffn_sb2", (2, D)),
]


def causal_mask_tiles():
    out = np.zeros((4, 128, 512), dtype=np.float32)
    k = np.arange(128)[:, None]
    q = np.arange(512)[None, :]
    for i in range(4):
        out[i] = np.where(i * 128 + k <= q, 0.0, -1e30)
    return out


def rope_epilogue(p, a, pref, costile, sintile, cosk, sink, out16, outk, npairs_off):
    t1 = a.sb(f"{pref}_t1", [128, 256], F32)
    t2 = a.sb(f"{pref}_t2", [128, 256], F32)

    def epi(p_, ps, pskey, tt_abs, ns_abs, nlen):
        po = npairs_off(ns_abs)
        npr = nlen // 2
        ev = ps[0:128, 0:nlen:2]
        od = ps[0:128, 1:nlen:2]
        cs = costile[:, po:po + npr]
        sn = sintile[:, po:po + npr]
        p_.op("vector", lambda v: v.tensor_mul(t1[:, 0:npr], ev, cs),
              reads=[pskey, cosk], writes=[f"{pref}_t1"])
        p_.op("vector", lambda v: v.tensor_mul(t2[:, 0:npr], od, sn),
              reads=[pskey, sink], writes=[f"{pref}_t2"])
        p_.op("vector", lambda v: v.tensor_sub(out16[:, 0:nlen:2], t1[:, 0:npr], t2[:, 0:npr]),
              reads=[f"{pref}_t1", f"{pref}_t2"], writes=[outk])
        p_.op("vector", lambda v: v.tensor_mul(t1[:, 0:npr], ev, sn),
              reads=[pskey, sink], writes=[f"{pref}_t1"])
        p_.op("vector", lambda v: v.tensor_mul(t2[:, 0:npr], od, cs),
              reads=[pskey, cosk], writes=[f"{pref}_t2"])
        p_.op("vector", lambda v: v.tensor_add(out16[:, 1:nlen:2], t1[:, 0:npr], t2[:, 0:npr]),
              reads=[f"{pref}_t1", f"{pref}_t2"], writes=[outk])
    return epi


def build_L0_attn(p, a, io, x_res, scratch):
    nc = p.nc
    H, HKV, DK = 16, 4, 64
    xn16 = scratch["xn16"]
    psb, psk = scratch["psb"], scratch["psk"]
    ident = scratch["ident_sb"]
    ones_row = scratch["ones_row"]
    zeros_row = scratch["zeros_row"]

    q16d = nc.dram_tensor("l0_q16", [T, D], BF16, kind="Internal")
    k16d = nc.dram_tensor("l0_k16", [T, 256], BF16, kind="Internal")
    scratch["_l0q16d"] = q16d
    scratch["_l0k16d"] = k16d
    ctxd = scratch["dram_bf16_DT"]

    cos_t, sin_t = rope_tables_L0(T, DK)
    cos_rep = nc.inline_tensor(np.tile(cos_t, (1, 16)).astype(ml_dtypes.bfloat16), "l0_cos")
    sin_rep = nc.inline_tensor(np.tile(sin_t, (1, 16)).astype(ml_dtypes.bfloat16), "l0_sin")
    masks_d = nc.inline_tensor(np.ascontiguousarray(
        np.transpose(causal_mask_tiles(), (1, 0, 2)).reshape(128, 4 * 512)), "l0_masks")

    with a.scope():
        ln_pass(p, a, x_res, xn16, "l0ln")

    with a.scope():
        v_sb = a.sb("l0_vsb", [128, NT * 260], BF16)
        p.op("vector", lambda v: v.memset(v_sb[:], 1.0), writes=["l0_vsb"])

        with a.scope():
            wq = load_weight_bf16(p, a, io["L0_attn_Wq"], "l0_wq")
            wk = load_weight_bf16(p, a, io["L0_attn_Wk"], "l0_wk")
            wv = load_weight_bf16(p, a, io["L0_attn_Wv"], "l0_wv")
            for kc in range(DC):
                p.op("vector", lambda v, kc=kc: v.tensor_scalar(wq[:, kc * D:(kc + 1) * D], wq[:, kc * D:(kc + 1) * D], float(DK) ** -0.5, None, ALU.mult),
                     reads=["l0_wq"], writes=["l0_wq"])
            xf = [a.sb(f"l0_xf{kc}", [128, T], BF16) for kc in range(DC)]
            xfk = [f"l0_xf{kc}" for kc in range(DC)]
            fm_loads(p, xf, xfk, xn16)
            costile = a.sb("l0_cost", [128, 512], BF16)
            sintile = a.sb("l0_sint", [128, 512], BF16)
            qro = a.sb("l0_qro", [128, 512], BF16)
            kro = a.sb("l0_kro", [128, 256], BF16)

            rope_q = rope_epilogue(p, a, "l0rq", costile, sintile, "l0_cost", "l0_sint",
                                   qro, "l0_qro", lambda ns: ns // 2)
            rope_k = rope_epilogue(p, a, "l0rk", costile, sintile, "l0_cost", "l0_sint",
                                   kro, "l0_kro", lambda ns: 0)

            for tt in range(NT):
                t0 = tt * 128
                p.op("sync", lambda s, t0=t0: s.dma_start(out=costile[:], in_=cos_rep[t0:t0 + 128, 0:512]),
                     reads=[], writes=["l0_cost"], inc=16)
                p.op("sync", lambda s, t0=t0: s.dma_start(out=sintile[:], in_=sin_rep[t0:t0 + 128, 0:512]),
                     reads=[], writes=["l0_sint"], inc=16)

                def epi_q(p_, ps, pskey, tt_abs, ns_abs, nlen, t0=t0):
                    rope_q(p_, ps, pskey, tt_abs, ns_abs, nlen)
                    p_.op("sync", lambda s: s.dma_start(out=q16d[tt_abs:tt_abs + 128, ns_abs:ns_abs + nlen], in_=qro[:, 0:nlen]),
                          reads=["l0_qro"], writes=["l0q16d"], inc=16)

                def epi_k(p_, ps, pskey, tt_abs, ns_abs, nlen, t0=t0):
                    rope_k(p_, ps, pskey, tt_abs, ns_abs, nlen)
                    p_.op("sync", lambda s: s.dma_start(out=k16d[tt_abs:tt_abs + 128, ns_abs:ns_abs + nlen], in_=kro[:, 0:nlen]),
                          reads=["l0_kro"], writes=["l0k16d"], inc=16)

                def epi_v(p_, ps, pskey, tt_abs, ns_abs, nlen):
                    ttl = tt_abs // 128
                    dst = v_sb[:, ttl * 260:(ttl + 1) * 260].rearrange("p (h c) -> p h c", c=65)[:, :, 0:64]
                    srcv = ps[0:128, 0:256].rearrange("p (h c) -> p h c", c=64)
                    p_.op("vector", lambda v: v.tensor_copy(dst, srcv),
                          reads=[pskey], writes=["l0_vsb"])

                proj_tok(p, xf, xfk, wq, "l0_wq", D, zeros_row, "zeros_row", scratch["ones_row16"], "ones_row16",
                         psb[0:2], psk[0:2], epi_q, t0=t0, tw=128, chunks_t0=0)
                proj_tok(p, xf, xfk, wk, "l0_wk", 256, zeros_row, "zeros_row", scratch["ones_row16"], "ones_row16",
                         psb[2:4], psk[2:4], epi_k, t0=t0, tw=128, chunks_t0=0)
                proj_tok(p, xf, xfk, wv, "l0_wv", 256, zeros_row, "zeros_row", scratch["ones_row16"], "ones_row16",
                         psb[4:6], psk[4:6], epi_v, t0=t0, tw=128, chunks_t0=0)

        with a.scope():
            kT = [a.sb(f"l0_kT{h}", [65, T], BF16) for h in range(HKV)]
            kfm2 = [a.sb(f"l0_kfm{c}", [128, T], BF16) for c in range(2)]
            for c in range(2):
                p.op("sync", lambda s, c=c: s.dma_start(out=kfm2[c][:], in_=k16d[:, c * 128:(c + 1) * 128], transpose=True),
                     reads=["l0k16d"], writes=[f"l0_kfm{c}"], inc=16)
            for h in range(HKV):
                c, ro = h // 2, (h % 2) * 64
                p.op("vector", lambda v, h=h, c=c, ro=ro: v.tensor_copy(kT[h][0:64, :], kfm2[c][ro:ro + 64, :]),
                     reads=[f"l0_kfm{c}"], writes=[f"l0_kT{h}"])
                p.op("vector", lambda v, h=h: v.memset(kT[h][64:65, :], 1.0),
                     reads=[], writes=[f"l0_kT{h}"])
            qfm = [a.sb(f"l0_qfm{c}", [128, T], BF16) for c in range(DC)]
            qfmk = [f"l0_qfm{c}" for c in range(DC)]
            for c in range(DC):
                p.op("sync", lambda s, c=c: s.dma_start(out=qfm[c][:], in_=q16d[:, c * 128:(c + 1) * 128], transpose=True),
                     reads=["l0q16d"], writes=[qfmk[c]], inc=16)
            masks_sb = a.sb("l0_msk", [128, 4 * 512], F32)
            p.op("sync", lambda s: s.dma_start(out=masks_sb[:], in_=masks_d[:]),
                 reads=[], writes=["l0_msk"], inc=16)
            mcand = a.sb("l0_mc", [128, 8], F32)
            mcol = a.sb("l0_m", [128, 1], F32)
            rhs_aug = a.sb("l0_qaug", [65, 512], BF16)
            PT = [a.sb(f"l0_pt{i}", [128, 512], BF16) for i in range(2)]
            srow = a.sb("l0_srow", [1, 512], F32)
            ctx_f = a.sb("l0_ctxf", [65, 512], F32)
            ctx16 = a.sb("l0_ctx16", [64, 512], BF16)
            m1c, m1k = p.cst(-1.0)

            for h in range(H):
                kv, qc, roff = h // 4, h // 2, (h % 2) * 64
                for qs in range(TT512):
                    p.op("vector", lambda v, qs=qs, qc=qc, roff=roff: v.tensor_copy(rhs_aug[0:64, :], qfm[qc][roff:roff + 64, qs * 512:(qs + 1) * 512]),
                         reads=[qfmk[qc]], writes=["l0_qaug"])
                    for j in range(4):
                        qt = qs * 4 + j
                        nch = qt // 4 + 1
                        for kc in range(nch):
                            b = psb[kc % 4]
                            bk = psk[kc % 4]
                            p.op("tensor", lambda t, j=j, kc=kc, b=b, kv=kv: t.matmul(
                                b[0:128, 0:512],
                                rhs_aug[0:64, j * 128:(j + 1) * 128],
                                kT[kv][0:64, kc * 512:(kc + 1) * 512],
                                start=True, stop=True),
                                reads=["l0_qaug", f"l0_kT{kv}"], writes=[bk], inc=1)
                            p.op("vector", lambda v, kc=kc, b=b: v.reduce_max(mcand[:, kc:kc + 1], b[0:128, 0:512], axis=AX.X),
                                 reads=[bk], writes=["l0_mc"])
                        p.op("vector", lambda v, nch=nch: v.reduce_max(mcol[:], mcand[:, 0:nch], axis=AX.X),
                             reads=["l0_mc"], writes=["l0_m"])
                        p.op("tensor", lambda t: t.transpose(psb[5][0:1, 0:128], mcol[:], ident[:]),
                             reads=["l0_m", "ident_sb"], writes=[psk[5]])
                        p.op("scalar", lambda s, j=j: s.activation(rhs_aug[64:65, j * 128:(j + 1) * 128], psb[5][0:1, 0:128], AF.Copy, scale=m1c[0:1, 0:1]),
                             reads=[psk[5], m1k], writes=["l0_qaug_m"])
                    nkc = 4 * qs + 4
                    for kc in range(nkc):
                        b, bk = psb[kc % 3], psk[kc % 3]
                        pt, ptk = PT[kc % 2], f"l0_pt{kc % 2}"
                        p.op("tensor", lambda t, kc=kc, b=b, kv=kv: t.matmul(
                            b[0:128, 0:512],
                            kT[kv][0:65, kc * 128:(kc + 1) * 128],
                            rhs_aug[0:65, :],
                            start=True, stop=True),
                            reads=[f"l0_kT{kv}", "l0_qaug", "l0_qaug_m"], writes=[bk], inc=1)
                        di = kc - 4 * qs
                        if di >= 0:
                            p.op("vector", lambda v, b=b, di=di: v.tensor_add(b[0:128, 0:512], b[0:128, 0:512], masks_sb[:, di * 512:(di + 1) * 512]),
                                 reads=["l0_msk", bk], writes=[bk])
                        p.op("scalar", lambda s, b=b, pt=pt: s.activation(pt[:], b[0:128, 0:512], AF.Exp),
                             reads=[bk], writes=[ptk])
                        p.op("tensor", lambda t, kc=kc, pt=pt, nkc=nkc, kv=kv: t.matmul(
                            psb[4][0:65, 0:512],
                            v_sb[:, kc * 260 + kv * 65: kc * 260 + (kv + 1) * 65],
                            pt[:],
                            start=(kc == 0), stop=(kc == nkc - 1)),
                            reads=["l0_vsb", ptk] + ([psk[4]] if kc == 0 else []),
                            writes=([psk[4]] if kc in (0, nkc - 1) else []),
                            inc=(1 if kc == nkc - 1 else 0))
                    p.op("scalar", lambda s: s.copy(ctx_f[:], psb[4][0:65, 0:512]),
                         reads=[psk[4]], writes=["l0_ctxf"])
                    p.op("vector", lambda v: v.reciprocal(srow[:], ctx_f[64:65, :]),
                         reads=["l0_ctxf"], writes=["l0_srow"])
                    p.op("tensor", lambda t: t.matmul(psb[5][0:128, 0:512], ones_row[0:1, 0:128], srow[:], start=True, stop=True),
                         reads=["l0_srow", "ones_row"], writes=[psk[5]], inc=1)
                    p.op("vector", lambda v: v.tensor_mul(ctx16[:], ctx_f[0:64, :], psb[5][0:64, 0:512]),
                         reads=["l0_ctxf", psk[5]], writes=["l0_ctx16"])
                    p.op("sync", lambda s, h=h, qs=qs: s.dma_start(out=ctxd[h * 64:(h + 1) * 64, qs * 512:(qs + 1) * 512], in_=ctx16[:]),
                         reads=["l0_ctx16"], writes=["l0ctxd"], inc=16)

        with a.scope():
            wo = load_weight_bf16(p, a, io["L0_attn_Wo"], "l0_wo")
            cf = [a.sb(f"l0_cf{c}", [128, T], BF16) for c in range(DC)]
            cfk = [f"l0_cf{c}" for c in range(DC)]
            for c in range(DC):
                p.op("sync", lambda s, c=c: s.dma_start(out=cf[c][:], in_=ctxd[c * 128:(c + 1) * 128, :]),
                     reads=["l0ctxd"], writes=[cfk[c]], inc=16)
            proj_tok(p, cf, cfk, wo, "l0_wo", D, zeros_row, "zeros_row", scratch["ones_row16"], "ones_row16",
                     psb, psk, accum_epilogue_tok(p, scratch, x_res))


def top2_weights(p, a, pref, gates_ps, gates_psk, wts, wts_key, tt):
    g = a.tensors[f"{pref}_g"]
    m1 = a.tensors[f"{pref}_m1"]
    m2 = a.tensors[f"{pref}_m2"]
    msk1 = a.tensors[f"{pref}_msk1"]
    msk2 = a.tensors[f"{pref}_msk2"]
    g2 = a.tensors[f"{pref}_g2"]
    p2 = a.tensors[f"{pref}_p2"]
    K1, K2, K3, K4, K5, K6, K7 = (f"{pref}_g", f"{pref}_m1", f"{pref}_m2",
                                  f"{pref}_msk1", f"{pref}_msk2", f"{pref}_g2", f"{pref}_p2")
    p.op("vector", lambda v: v.tensor_copy(g[:], gates_ps[0:128, 0:8]),
         reads=[gates_psk], writes=[K1])
    p.op("vector", lambda v: v.reduce_max(m1[:], g[:], axis=AX.X), reads=[K1], writes=[K2])
    p.op("vector", lambda v: v.tensor_scalar(msk1[:], g[:], m1[:], None, ALU.is_equal),
         reads=[K1, K2], writes=[K4])
    p.op("vector", lambda v: v.scalar_tensor_tensor(g2[:], msk1[:], -1e30, g[:], ALU.mult, ALU.add),
         reads=[K4, K1], writes=[K6])
    p.op("vector", lambda v: v.reduce_max(m2[:], g2[:], axis=AX.X), reads=[K6], writes=[K3])
    p.op("vector", lambda v: v.tensor_scalar(msk2[:], g2[:], m2[:], None, ALU.is_equal),
         reads=[K6, K3], writes=[K5])
    p.op("vector", lambda v: v.tensor_sub(m2[:], m2[:], m1[:]), reads=[K3, K2], writes=[K3])
    p.op("scalar", lambda s: s.activation(p2[:], m2[:], AF.Sigmoid), reads=[K3], writes=[K7])
    p.op("vector", lambda v: v.tensor_scalar(m1[:], p2[:], -1.0, 1.0, ALU.mult, ALU.add),
         reads=[K7], writes=[K2])
    p.op("vector", lambda v, tt=tt: v.tensor_scalar(wts[:, tt * 8:(tt + 1) * 8], msk1[:], m1[:], None, ALU.mult),
         reads=[K4, K2], writes=[wts_key])
    p.op("vector", lambda v, tt=tt: v.scalar_tensor_tensor(wts[:, tt * 8:(tt + 1) * 8], msk2[:], p2[:], wts[:, tt * 8:(tt + 1) * 8], ALU.mult, ALU.add),
         reads=[K5, K7, wts_key], writes=[wts_key])


def build_L0_ffn(p, a, io, x_res, scratch, pref="l0f", gelu_fn=AF.Gelu,
                 wnames=("L0_ffn_rW", "L0_ffn_rb", "L0_ffn_W1", "L0_ffn_b1",
                         "L0_ffn_W2", "L0_ffn_b2", "L0_ffn_sW1", "L0_ffn_sb1",
                         "L0_ffn_sW2", "L0_ffn_sb2"), shared=True, lngb=None):
    nc = p.nc
    xn16 = scratch["xn16"]
    psb, psk = scratch["psb"], scratch["psk"]
    ones_row = scratch["ones_row"]
    zeros_row = scratch["zeros_row"]
    HID = 4096
    HC = HID // 128

    def _fm(chunks, keys, t0=0, tw=T):
        if lngb is None:
            fm_loads(p, chunks, keys, xn16, t0=t0, tw=tw)
        else:
            fm_loads_gb(p, chunks, keys, xn16, lngb[0], lngb[1], lngb[2], lngb[3], t0=t0, tw=tw)

    with a.scope():
        ln_pass(p, a, x_res, xn16, pref + "ln")

    with a.scope():
        wts = a.sb(f"{pref}_wts", [128, NT * 8], F32)
        with a.scope():
            rw = load_weight_bf16(p, a, io[wnames[0]], f"{pref}_rw")
            if wnames[1] is not None:
                rb, rbk = load_row(p, a, io[wnames[1]], f"{pref}_rb"), f"{pref}_rb"
                ones_g, ones_gk = ones_row, "ones_row"
            else:
                rb, rbk = zeros_row, "zeros_row"
                ones_g, ones_gk = scratch["ones_row16"], "ones_row16"
            xf = [a.sb(f"{pref}_gxf{kc}", [128, T], BF16) for kc in range(DC)]
            xfk = [f"{pref}_gxf{kc}" for kc in range(DC)]
            _fm(xf, xfk)
            for nm, shp in [("g", [128, 8]), ("m1", [128, 1]), ("m2", [128, 1]),
                            ("msk1", [128, 8]), ("msk2", [128, 8]), ("g2", [128, 8]),
                            ("p2", [128, 1])]:
                a.sb(f"{pref}_{nm}", shp, F32)

            def epi_g(p_, ps, pskey, tt_abs, ns_abs, nlen):
                top2_weights(p_, a, pref, ps, pskey, wts, f"{pref}_wts", tt_abs // 128)

            proj_tok(p, xf, xfk, rw, f"{pref}_rw", 8, rb, rbk, ones_g, ones_gk,
                     psb, psk, epi_g)

        experts = [("r", e) for e in range(8)]
        if shared:
            experts += [("s", e) for e in range(2)]
        w1_all, b1_all, w2_all, b2_all = (io[wnames[2]], io[wnames[3]], io[wnames[4]], io[wnames[5]])
        if shared:
            sw1, sb1, sw2, sb2 = (io[wnames[6]], io[wnames[7]], io[wnames[8]], io[wnames[9]])
        else:
            sw1 = sb1 = sw2 = sb2 = None

        w1s = a.sb(f"{pref}_w1", [128, DC * HID], BF16)
        w2s = a.sb(f"{pref}_w2", [128, HC * D], BF16)
        b1r = a.sb(f"{pref}_b1", [1, HID], BF16)
        b2r = a.sb(f"{pref}_b2", [1, D], BF16)
        ones16 = scratch["ones_row16"]
        hsb = a.sb(f"{pref}_h", [128, HC * 512], BF16)
        xfs2 = [[a.sb(f"{pref}_xf{par}_{kc}", [128, 512], BF16) for kc in range(DC)] for par in range(2)]
        xfsk2 = [[f"{pref}_xf{par}_{kc}" for kc in range(DC)] for par in range(2)]
        ob = [a.sb(f"{pref}_ob{i}", [128, 512], F32) for i in range(1)]
        obc = [0]

        for kind, e in experts:
            w1d = w1_all[e] if kind == "r" else sw1[e]
            w2d = w2_all[e] if kind == "r" else sw2[e]
            b1d = b1_all[e] if kind == "r" else sb1[e]
            b2d = b2_all[e] if kind == "r" else sb2[e]
            w1v = w1d.rearrange("(kc p) n -> p kc n", p=128)
            p.op("gpsimd", lambda g, w1v=w1v: g.dma_start(out=w1s[:].rearrange("p (kc n) -> p kc n", kc=DC), in_=w1v),
                 reads=[w1_all.name if kind == "r" else sw1.name], writes=[f"{pref}_w1"], inc=16)
            w2v = w2d.rearrange("(kc p) n -> p kc n", p=128)
            p.op("gpsimd", lambda g, w2v=w2v: g.dma_start(out=w2s[:].rearrange("p (kc n) -> p kc n", kc=HC), in_=w2v),
                 reads=[w2_all.name if kind == "r" else sw2.name], writes=[f"{pref}_w2"], inc=16)
            b1row = (b1_all if kind == "r" else sb1)[e:e + 1, :]
            b2row = (b2_all if kind == "r" else sb2)[e:e + 1, :]
            p.op("gpsimd", lambda g, b1row=b1row: g.dma_start(out=b1r[:], in_=b1row),
                 reads=[], writes=[f"{pref}_b1"], inc=16)
            p.op("gpsimd", lambda g, b2row=b2row: g.dma_start(out=b2r[:], in_=b2row),
                 reads=[], writes=[f"{pref}_b2"], inc=16)

            for ts in range(TT512):
                t0 = ts * 512
                xfs, xfsk = xfs2[ts % 2], xfsk2[ts % 2]
                _fm(xfs, xfsk, t0=t0, tw=512)

                def epi_h(p_, ps, pskey, nc_abs, tt_abs, tlen, nlen):
                    hc = nc_abs // 128
                    p_.op("scalar", lambda s, hc=hc: s.activation(hsb[:, hc * 512:(hc + 1) * 512], ps[0:128, 0:512], gelu_fn),
                          reads=[pskey], writes=[f"{pref}_h"])

                proj_fm(p, xfs, xfsk, w1s, f"{pref}_w1", HID, b1r, f"{pref}_b1",
                        ones16, psb[0:4], psk[0:4], epi_h, t0=t0, tw=512)

                hch = [hsb[:, hc * 512:(hc + 1) * 512] for hc in range(HC)]

                def epi_y(p_, ps, pskey, tt_abs, ns_abs, nlen, kind=kind, e=e):
                    ttl = tt_abs // 128
                    i = 0
                    o, ok_ = ob[i], f"{pref}_ob{i}"
                    if kind == "r":
                        p_.op("vector", lambda v: v.tensor_scalar(o[:, 0:nlen], ps[0:128, 0:nlen], wts[:, ttl * 8 + e: ttl * 8 + e + 1], None, ALU.mult),
                              reads=[pskey, f"{pref}_wts"], writes=[ok_])
                    else:
                        p_.op("vector", lambda v: v.tensor_scalar(o[:, 0:nlen], ps[0:128, 0:nlen], 0.5, None, ALU.mult),
                              reads=[pskey], writes=[ok_])
                    p_.op("gpsimd", lambda g: g.dma_start(out=x_res[tt_abs:tt_abs + 128, ns_abs:ns_abs + nlen], in_=o[0:128, 0:nlen], accum_op=ALU.add),
                          reads=[ok_], writes=[x_res.name], inc=16, accum=True)

                for ti in range(4):
                    tt_abs = t0 + ti * 128
                    for nsl in range(2):
                        ns_abs = nsl * 512
                        bqi = (ti * 2 + nsl) % 4
                        ps, pskey = psb[4 + bqi % 2], psk[4 + bqi % 2]
                        first = True
                        for hc in range(HC):
                            p.op("tensor", lambda t, hc=hc, ti=ti, ns_abs=ns_abs, ps=ps, first=first:
                                 t.matmul(ps[0:128, 0:512],
                                          hsb[:, hc * 512 + ti * 128: hc * 512 + ti * 128 + 128],
                                          w2s[:, hc * D + ns_abs: hc * D + ns_abs + 512],
                                          start=first, stop=False),
                                 reads=[f"{pref}_h", f"{pref}_w2"] + ([pskey] if first else []),
                                 writes=([pskey] if first else []), inc=0)
                            first = False
                        p.op("tensor", lambda t, ns_abs=ns_abs, ps=ps: t.matmul(
                            ps[0:128, 0:512], ones16[0:1, 0:128], b2r[0:1, ns_abs:ns_abs + 512],
                            start=False, stop=True),
                            reads=[f"{pref}_b2", "ones_row16"], writes=[pskey], inc=1)
                        epi_y(p, ps, pskey, tt_abs, ns_abs, 512)




def row_to_cols(p, a, row_sb, row_key, name, ident, psb0, psk0, n=D):
    ncc = n // 128
    col = a.sb(name, [128, ncc], F32)
    for kc in range(ncc):
        p.op("tensor", lambda t, kc=kc: t.transpose(psb0[0:128, 0:1], row_sb[0:1, kc * 128:(kc + 1) * 128], ident[0:1, 0:1]),
             reads=[row_key, "ident_sb"], writes=[psk0])
        p.op("scalar", lambda s, kc=kc: s.copy(col[:, kc:kc + 1], psb0[0:128, 0:1]),
             reads=[psk0], writes=[name])
    return col


def fm_loads_gb(p, chunks_sb, keys, src16, gcol, gkey, bcol, bkey, t0=0, tw=T):
    for kc in range(len(chunks_sb)):
        cs = chunks_sb[kc]
        p.op("sync", lambda s, cs=cs, kc=kc: s.dma_start(
            out=cs[:, 0:tw], in_=src16[t0:t0 + tw, kc * 128:(kc + 1) * 128], transpose=True),
            reads=[src16.name], writes=[keys[kc]], inc=16)
        p.op("vector", lambda v, cs=cs, kc=kc: v.tensor_scalar(
            cs[:, 0:tw], cs[:, 0:tw], gcol[:, kc:kc + 1], bcol[:, kc:kc + 1], ALU.mult, ALU.add),
            reads=[keys[kc], gkey, bkey], writes=[keys[kc]])


def attn_core(p, a, scratch, pref, q16d, qkey, k16d, kkey, v16d, vkey, ctxd, ctxkey, H=16):
    nc = p.nc
    psb, psk = scratch["psb"], scratch["psk"]
    ident = scratch["ident_sb"]
    ones_row = scratch["ones_row"]
    DKH = 64
    with a.scope():
        kfm = [a.sb(f"{pref}_kfm{c}", [128, T], BF16) for c in range(H // 2)]
        for c in range(H // 2):
            p.op("sync", lambda s, c=c: s.dma_start(out=kfm[c][:], in_=k16d[:, c * 128:(c + 1) * 128], transpose=True),
                 reads=[kkey], writes=[f"{pref}_kfm{c}"], inc=16)
        kT_h = a.sb(f"{pref}_kT", [65, T], BF16)
        v_aug = a.sb(f"{pref}_vaug", [128, NT * 65], BF16)
        qfm = [a.sb(f"{pref}_qfm{c}", [128, T], BF16) for c in range(H // 2)]
        rhs_aug = [a.sb(f"{pref}_qaug{i}", [65, 512], BF16) for i in range(2)]
        PT = [a.sb(f"{pref}_pt{i}", [128, 512], BF16) for i in range(4)]
        mcand = [a.sb(f"{pref}_mc{i}", [128, 8], F32) for i in range(2)]
        mcol = [a.sb(f"{pref}_m{i}", [128, 1], F32) for i in range(2)]
        srow = [a.sb(f"{pref}_srow{i}", [1, 512], F32) for i in range(2)]
        ctx_f = [a.sb(f"{pref}_ctxf{i}", [65, 512], F32) for i in range(2)]
        ctx16 = [a.sb(f"{pref}_ctx16{i}", [64, 512], BF16) for i in range(2)]
        m1c, m1k = p.cst(-1.0)
        for c in range(H // 2):
            p.op("sync", lambda s, c=c: s.dma_start(out=qfm[c][:], in_=q16d[:, c * 128:(c + 1) * 128], transpose=True),
                 reads=[qkey], writes=[f"{pref}_qfm{c}"], inc=16)
        p.op("vector", lambda v: v.memset(kT_h[64:65, :], 1.0), writes=[f"{pref}_kT"])
        p.op("vector", lambda v: v.memset(v_aug[:].rearrange("p (tt c) -> p tt c", c=65)[:, :, 64:65], 1.0),
             writes=[f"{pref}_vaug"])

        for h in range(H):
            c, ro = h // 2, (h % 2) * 64
            p.op("vector", lambda v, c=c, ro=ro: v.tensor_copy(kT_h[0:64, :], kfm[c][ro:ro + 64, :]),
                 reads=[f"{pref}_kfm{c}"], writes=[f"{pref}_kT"])
            vv = v16d.rearrange("(tt p) n -> p tt n", p=128)
            p.op("sync", lambda s, h=h, vv=vv: s.dma_start(
                out=v_aug[:].rearrange("p (tt c) -> p tt c", c=65)[:, :, 0:64],
                in_=vv[:, :, h * 64:(h + 1) * 64]),
                reads=[vkey], writes=[f"{pref}_vaug"], inc=16)
            for qs in range(TT512):
                pid = qs % 2
                ra, KA = rhs_aug[pid], f"{pref}_qaug{pid}"
                mca, KMC = mcand[pid], f"{pref}_mc{pid}"
                mco, KM = mcol[pid], f"{pref}_m{pid}"
                sro, KS = srow[pid], f"{pref}_srow{pid}"
                cfb, KC_ = ctx_f[pid], f"{pref}_ctxf{pid}"
                c16, KC16 = ctx16[pid], f"{pref}_ctx16{pid}"
                ctxbank, ctxbk = psb[4 + 2 * pid], psk[4 + 2 * pid]
                xbank, xbk = psb[5 + 2 * pid], psk[5 + 2 * pid]
                p.op("vector", lambda v, c=c, ro=ro, ra=ra, qs=qs: v.tensor_copy(ra[0:64, :], qfm[c][ro:ro + 64, qs * 512:(qs + 1) * 512]),
                     reads=[f"{pref}_qfm{c}"], writes=[KA])
                for j in range(4):
                    for kc in range(TT512):
                        b, bk = psb[kc % 4], psk[kc % 4]
                        p.op("tensor", lambda t, j=j, kc=kc, b=b, ra=ra: t.matmul(
                            b[0:128, 0:512], ra[0:64, j * 128:(j + 1) * 128],
                            kT_h[0:64, kc * 512:(kc + 1) * 512], start=True, stop=True),
                            reads=[KA, f"{pref}_kT"], writes=[bk], inc=1)
                        p.op("vector", lambda v, kc=kc, b=b, mca=mca: v.reduce_max(mca[:, kc:kc + 1], b[0:128, 0:512], axis=AX.X),
                             reads=[bk], writes=[KMC])
                    p.op("vector", lambda v, mca=mca, mco=mco: v.reduce_max(mco[:], mca[:], axis=AX.X),
                         reads=[KMC], writes=[KM])
                    p.op("tensor", lambda t, mco=mco, xbank=xbank: t.transpose(xbank[0:1, 0:128], mco[:], ident[:]),
                         reads=[KM, "ident_sb"], writes=[xbk])
                    p.op("scalar", lambda s, j=j, ra=ra, xbank=xbank: s.activation(ra[64:65, j * 128:(j + 1) * 128], xbank[0:1, 0:128], AF.Copy, scale=m1c[0:1, 0:1]),
                         reads=[xbk, m1k], writes=[KA + "m"])
                for kc in range(NT):
                    b, bk = psb[kc % 3], psk[kc % 3]
                    pt, ptk = PT[2 * pid + kc % 2], f"{pref}_pt{2 * pid + kc % 2}"
                    p.op("tensor", lambda t, kc=kc, b=b, ra=ra: t.matmul(
                        b[0:128, 0:512], kT_h[0:65, kc * 128:(kc + 1) * 128],
                        ra[0:65, :], start=True, stop=True),
                        reads=[f"{pref}_kT", KA, KA + "m"], writes=[bk], inc=1)
                    p.op("scalar", lambda s, b=b, pt=pt: s.activation(pt[:], b[0:128, 0:512], AF.Exp),
                         reads=[bk], writes=[ptk])
                    p.op("tensor", lambda t, kc=kc, pt=pt, ctxbank=ctxbank: t.matmul(
                        ctxbank[0:65, 0:512],
                        v_aug[:, kc * 65:(kc + 1) * 65],
                        pt[:], start=(kc == 0), stop=(kc == NT - 1)),
                        reads=[f"{pref}_vaug", ptk] + ([ctxbk] if kc == 0 else []),
                        writes=([ctxbk] if kc in (0, NT - 1) else []),
                        inc=(1 if kc == NT - 1 else 0))
                p.op("scalar", lambda s, cfb=cfb, ctxbank=ctxbank: s.copy(cfb[:], ctxbank[0:65, 0:512]),
                     reads=[ctxbk], writes=[KC_])
                p.op("vector", lambda v, sro=sro, cfb=cfb: v.reciprocal(sro[:], cfb[64:65, :]),
                     reads=[KC_], writes=[KS])
                p.op("tensor", lambda t, sro=sro, xbank=xbank: t.matmul(xbank[0:128, 0:512], ones_row[0:1, 0:128], sro[:], start=True, stop=True),
                     reads=[KS, "ones_row"], writes=[xbk], inc=1)
                p.op("vector", lambda v, c16=c16, cfb=cfb, xbank=xbank: v.tensor_mul(c16[:], cfb[0:64, :], xbank[0:64, 0:512]),
                     reads=[KC_, xbk], writes=[KC16])
                p.op("sync", lambda s, h=h, qs=qs, c16=c16: s.dma_start(out=ctxd[h * 64:(h + 1) * 64, qs * 512:(qs + 1) * 512], in_=c16[:]),
                     reads=[KC16], writes=[ctxkey], inc=16)


L2_WEIGHTS = [
    ("L2_attn_Wq", (D, D)), ("L2_attn_bq", (D,)), ("L2_attn_Wk", (D, D)),
    ("L2_attn_bk", (D,)), ("L2_attn_Wv", (D, D)), ("L2_attn_bv", (D,)),
    ("L2_ffn_ln1_w", (D,)), ("L2_ffn_ln1_b", (D,)), ("L2_ffn_ln2_w", (D,)),
    ("L2_ffn_ln2_b", (D,)), ("L2_ffn_w_qkv", (D, 3 * D)), ("L2_ffn_b_qkv", (3 * D,)),
    ("L2_ffn_w_o", (D, D)), ("L2_ffn_b_o", (D,)), ("L2_ffn_w_gate", (D, 8)),
    ("L2_ffn_w1", (8, D, 4096)), ("L2_ffn_b1", (8, 4096)),
    ("L2_ffn_w2", (8, 4096, D)), ("L2_ffn_b2", (8, D)),
]


def ctx_residual_add(p, a, scratch, pref, ctxd, ctxkey, dst_res):
    with a.scope():
        ct = a.sb(f"{pref}_ct", [128, D], BF16)
        cf = a.sb(f"{pref}_cf", [128, D], F32)
        for tt in range(NT):
            p.op("sync", lambda s, tt=tt: s.dma_start(out=ct[:], in_=ctxd[0:D, tt * 128:(tt + 1) * 128], transpose=True),
                 reads=[ctxkey], writes=[f"{pref}_ct"], inc=16)
            p.op("vector", lambda v: v.tensor_copy(cf[:], ct[:]),
                 reads=[f"{pref}_ct"], writes=[f"{pref}_cf"])
            p.op("gpsimd", lambda g, tt=tt: g.dma_start(out=dst_res[tt * 128:(tt + 1) * 128, :], in_=cf[:], accum_op=ALU.add),
                 reads=[f"{pref}_cf"], writes=[dst_res.name], inc=16, accum=True)


def build_L2_attn(p, a, io, x_res, scratch):
    nc = p.nc
    xn16 = scratch["xn16"]
    psb, psk = scratch["psb"], scratch["psk"]
    ones_row = scratch["ones_row"]
    zeros_row = scratch["zeros_row"]
    DK = 64

    q16d = nc.dram_tensor("l2_q16", [T, D], BF16, kind="Internal")
    k16d = nc.dram_tensor("l2_k16", [T, D], BF16, kind="Internal")
    v16d = nc.dram_tensor("l2_v16", [T, D], BF16, kind="Internal")
    ctxd = scratch["dram_bf16_DT"]

    cos_pos, sin_pos = rope_tables_L2(T, DK)
    cos_pair = np.ascontiguousarray(cos_pos[:, 0::2])
    sin_pair = np.ascontiguousarray(sin_pos[:, 0::2])
    cos_rep = nc.inline_tensor(np.tile(cos_pair, (1, 16)).astype(ml_dtypes.bfloat16), "l2_cos")
    sin_rep = nc.inline_tensor(np.tile(sin_pair, (1, 16)).astype(ml_dtypes.bfloat16), "l2_sin")

    with a.scope():
        ln_pass(p, a, x_res, xn16, "l2ln")

    with a.scope():
        with a.scope():
            wq = load_weight_bf16(p, a, io["L2_attn_Wq"], "l2_wq")
            wk = load_weight_bf16(p, a, io["L2_attn_Wk"], "l2_wk")
            wv = load_weight_bf16(p, a, io["L2_attn_Wv"], "l2_wv")
            bq = load_row(p, a, io["L2_attn_bq"], "l2_bq")
            bk_ = load_row(p, a, io["L2_attn_bk"], "l2_bk")
            bv = load_row(p, a, io["L2_attn_bv"], "l2_bv")
            for kc in range(DC):
                p.op("vector", lambda v, kc=kc: v.tensor_scalar(wq[:, kc * D:(kc + 1) * D], wq[:, kc * D:(kc + 1) * D], float(DK) ** -0.5, None, ALU.mult),
                     reads=["l2_wq"], writes=["l2_wq"])
            p.op("vector", lambda v: v.tensor_scalar(bq[:], bq[:], float(DK) ** -0.5, None, ALU.mult),
                 reads=["l2_bq"], writes=["l2_bq"])
            xf = [a.sb(f"l2_xf{kc}", [128, T], BF16) for kc in range(DC)]
            xfk = [f"l2_xf{kc}" for kc in range(DC)]
            fm_loads(p, xf, xfk, xn16)
            costile = a.sb("l2_cost", [128, 512], BF16)
            sintile = a.sb("l2_sint", [128, 512], BF16)
            qro = a.sb("l2_qro", [128, 512], BF16)
            kro = a.sb("l2_kro", [128, 512], BF16)
            vro = a.sb("l2_vro", [128, 512], BF16)
            rope_q = rope_epilogue(p, a, "l2rq", costile, sintile, "l2_cost", "l2_sint",
                                   qro, "l2_qro", lambda ns: ns // 2)
            rope_k = rope_epilogue(p, a, "l2rk", costile, sintile, "l2_cost", "l2_sint",
                                   kro, "l2_kro", lambda ns: ns // 2)

            for tt in range(NT):
                t0 = tt * 128
                p.op("sync", lambda s, t0=t0: s.dma_start(out=costile[:], in_=cos_rep[t0:t0 + 128, 0:512]),
                     reads=[], writes=["l2_cost"], inc=16)
                p.op("sync", lambda s, t0=t0: s.dma_start(out=sintile[:], in_=sin_rep[t0:t0 + 128, 0:512]),
                     reads=[], writes=["l2_sint"], inc=16)

                def epi_q(p_, ps, pskey, tt_abs, ns_abs, nlen):
                    rope_q(p_, ps, pskey, tt_abs, ns_abs, nlen)
                    p_.op("sync", lambda s: s.dma_start(out=q16d[tt_abs:tt_abs + 128, ns_abs:ns_abs + nlen], in_=qro[:, 0:nlen]),
                          reads=["l2_qro"], writes=["l2q16d"], inc=16)

                def epi_k(p_, ps, pskey, tt_abs, ns_abs, nlen):
                    rope_k(p_, ps, pskey, tt_abs, ns_abs, nlen)
                    p_.op("sync", lambda s: s.dma_start(out=k16d[tt_abs:tt_abs + 128, ns_abs:ns_abs + nlen], in_=kro[:, 0:nlen]),
                          reads=["l2_kro"], writes=["l2k16d"], inc=16)

                def epi_v(p_, ps, pskey, tt_abs, ns_abs, nlen):
                    p_.op("vector", lambda v: v.tensor_copy(vro[:, 0:nlen], ps[0:128, 0:nlen]),
                          reads=[pskey], writes=["l2_vro"])
                    p_.op("sync", lambda s: s.dma_start(out=v16d[tt_abs:tt_abs + 128, ns_abs:ns_abs + nlen], in_=vro[:, 0:nlen]),
                          reads=["l2_vro"], writes=["l2v16d"], inc=16)

                proj_tok(p, xf, xfk, wq, "l2_wq", D, bq, "l2_bq", ones_row, "ones_row",
                         psb[0:2], psk[0:2], epi_q, t0=t0, tw=128, chunks_t0=0)
                proj_tok(p, xf, xfk, wk, "l2_wk", D, bk_, "l2_bk", ones_row, "ones_row",
                         psb[2:4], psk[2:4], epi_k, t0=t0, tw=128, chunks_t0=0)
                proj_tok(p, xf, xfk, wv, "l2_wv", D, bv, "l2_bv", ones_row, "ones_row",
                         psb[4:6], psk[4:6], epi_v, t0=t0, tw=128, chunks_t0=0)

        attn_core(p, a, scratch, "l2a", q16d, "l2q16d", k16d, "l2k16d", v16d, "l2v16d",
                  ctxd, "l2ctxd", H=16)
        ctx_residual_add(p, a, scratch, "l2r", ctxd, "l2ctxd", x_res)


def build_L2_ffn(p, a, io, x_res, scratch):
    nc = p.nc
    xn16 = scratch["xn16"]
    psb, psk = scratch["psb"], scratch["psk"]
    ident = scratch["ident_sb"]
    ones_row = scratch["ones_row"]
    zeros_row = scratch["zeros_row"]
    DK = 64

    x2_res = nc.dram_tensor("l2f_x2", [T, D], F32, kind="Internal")
    q16d = nc.dram_tensor("l2f_q16", [T, D], BF16, kind="Internal")
    k16d = nc.dram_tensor("l2f_k16", [T, D], BF16, kind="Internal")
    v16d = nc.dram_tensor("l2f_v16", [T, D], BF16, kind="Internal")
    ctxd = scratch["dram_bf16_DT"]

    with a.scope():
        ln_pass(p, a, x_res, xn16, "l2f0ln", dst32=x2_res)

    with a.scope():
        ln1w = load_row(p, a, io["L2_ffn_ln1_w"], "l2f_ln1w")
        ln1b = load_row(p, a, io["L2_ffn_ln1_b"], "l2f_ln1b")
        g1 = row_to_cols(p, a, ln1w, "l2f_ln1w", "l2f_g1", ident, psb[6], psk[6])
        b1 = row_to_cols(p, a, ln1b, "l2f_ln1b", "l2f_b1c", ident, psb[6], psk[6])
        with a.scope():
            ln_pass(p, a, x2_res, xn16, "l2f1ln")
        with a.scope():
            wqkv = load_weight_bf16(p, a, io["L2_ffn_w_qkv"], "l2f_wqkv")
            bqkv = load_row(p, a, io["L2_ffn_b_qkv"], "l2f_bqkv")
            for kc in range(DC):
                p.op("vector", lambda v, kc=kc: v.tensor_scalar(wqkv[:, kc * 3 * D: kc * 3 * D + D], wqkv[:, kc * 3 * D: kc * 3 * D + D], float(DK) ** -0.5, None, ALU.mult),
                     reads=["l2f_wqkv"], writes=["l2f_wqkv"])
            p.op("vector", lambda v: v.tensor_scalar(bqkv[:, 0:D], bqkv[:, 0:D], float(DK) ** -0.5, None, ALU.mult),
                 reads=["l2f_bqkv"], writes=["l2f_bqkv"])
            xf = [a.sb(f"l2f_xf{kc}", [128, T], BF16) for kc in range(DC)]
            xfk = [f"l2f_xf{kc}" for kc in range(DC)]
            fm_loads_gb(p, xf, xfk, xn16, g1, "l2f_g1", b1, "l2f_b1c")
            qkv16 = a.sb("l2f_qkv16", [128, 512], BF16)

            def epi_qkv(p_, ps, pskey, tt_abs, ns_abs, nlen):
                p_.op("vector", lambda v: v.tensor_copy(qkv16[:, 0:nlen], ps[0:128, 0:nlen]),
                      reads=[pskey], writes=["l2f_qkv16"])
                if ns_abs < D:
                    dst, dk_ = q16d, "l2fq16d"
                    col = ns_abs
                elif ns_abs < 2 * D:
                    dst, dk_ = k16d, "l2fk16d"
                    col = ns_abs - D
                else:
                    dst, dk_ = v16d, "l2fv16d"
                    col = ns_abs - 2 * D
                p_.op("sync", lambda s, dst=dst, col=col: s.dma_start(out=dst[tt_abs:tt_abs + 128, col:col + nlen], in_=qkv16[:, 0:nlen]),
                      reads=["l2f_qkv16"], writes=[dk_], inc=16)

            proj_tok(p, xf, xfk, wqkv, "l2f_wqkv", 3 * D, bqkv, "l2f_bqkv", ones_row, "ones_row",
                     psb[0:4], psk[0:4], epi_qkv)

        attn_core(p, a, scratch, "l2fa", q16d, "l2fq16d", k16d, "l2fk16d", v16d, "l2fv16d",
                  ctxd, "l2fctxd", H=16)

        with a.scope():
            wo = load_weight_bf16(p, a, io["L2_ffn_w_o"], "l2f_wo")
            bo = load_row(p, a, io["L2_ffn_b_o"], "l2f_bo")
            cf = [a.sb(f"l2f_cf{c}", [128, T], BF16) for c in range(DC)]
            cfk = [f"l2f_cf{c}" for c in range(DC)]
            for c in range(DC):
                p.op("sync", lambda s, c=c: s.dma_start(out=cf[c][:], in_=ctxd[c * 128:(c + 1) * 128, :]),
                     reads=["l2fctxd"], writes=[cfk[c]], inc=16)
            proj_tok(p, cf, cfk, wo, "l2f_wo", D, bo, "l2f_bo", ones_row, "ones_row",
                     psb, psk, accum_epilogue_tok(p, scratch, x2_res))

    with a.scope():
        ln2w = load_row(p, a, io["L2_ffn_ln2_w"], "l2f_ln2w")
        ln2b = load_row(p, a, io["L2_ffn_ln2_b"], "l2f_ln2b")
        g2 = row_to_cols(p, a, ln2w, "l2f_ln2w", "l2f_g2", ident, psb[6], psk[6])
        b2c = row_to_cols(p, a, ln2b, "l2f_ln2b", "l2f_b2c", ident, psb[6], psk[6])
        build_L0_ffn(p, a, io, x2_res, scratch, pref="l2m", gelu_fn=AF.Gelu_apprx_tanh,
                     wnames=("L2_ffn_w_gate", None, "L2_ffn_w1", "L2_ffn_b1",
                             "L2_ffn_w2", "L2_ffn_b2", None, None, None, None),
                     shared=False, lngb=(g2, "l2f_g2", b2c, "l2f_b2c"))

    with a.scope():
        xt = a.sb("l2f_xt", [128, D], F32)
        for tt in range(NT):
            p.op("sync", lambda s, tt=tt: s.dma_start(out=xt[:], in_=x2_res[tt * 128:(tt + 1) * 128, :]),
                 reads=[x2_res.name], writes=["l2f_xt"], inc=16)
            p.op("gpsimd", lambda g, tt=tt: g.dma_start(out=x_res[tt * 128:(tt + 1) * 128, :], in_=xt[:], accum_op=ALU.add),
                 reads=["l2f_xt"], writes=[x_res.name], inc=16, accum=True)





L3_WEIGHTS = [
    ("L3_attn_Wq", (D, D)), ("L3_attn_bq", (D,)), ("L3_attn_Wk", (D, D)),
    ("L3_attn_bk", (D,)), ("L3_attn_Wv", (D, D)), ("L3_attn_bv", (D,)),
    ("L3_attn_Wo", (D, D)), ("L3_attn_bo", (D,)), ("L3_attn_Wproj", (D, D)),
    ("L3_attn_bproj", (D,)), ("L3_attn_W1", (512, D)), ("L3_attn_b1", (512,)),
    ("L3_attn_W2", (1, 512)), ("L3_attn_b2", (1,)),
    ("L3_ffn_weight", (D, D)), ("L3_ffn_alpha", (1,)), ("L3_ffn_bias", (D,)),
]


def load_weight_T_bf16(p, a, w_dram, wkey, name, stage_d, stage_key):
    O, I = w_dram.shape
    IC = I // 128
    stage_key = "wstage"
    p.op("gpsimd", lambda g: g.dma_start(out=stage_d[0:O, 0:I], in_=w_dram[:]),
         reads=[wkey], writes=[stage_key], inc=16)
    ws = a.sb(name, [128, IC * O], BF16)
    for kc in range(IC):
        p.op("sync", lambda s, kc=kc: s.dma_start(
            out=ws[:, kc * O:(kc + 1) * O], in_=stage_d[0:O, kc * 128:(kc + 1) * 128], transpose=True),
            reads=[stage_key], writes=[name], inc=16)
    return ws


def allreduce_scalar(p, a, pref, cc_in_d, cc_out_d, src_sb, src_key, dst_sb, dst_key, sem):
    p.op("gpsimd", lambda g: g.dma_start(out=cc_in_d[:], in_=src_sb[:]),
         reads=[src_key], writes=[pref + "_ccin"], inc=16)
    p.op("gpsimd", lambda g: g.collective_compute(
        "AllReduce", ALU.add, replica_groups=[[0, 1, 2, 3, 4, 5, 6, 7]],
        ins=[cc_in_d.ap().opt()], outs=[cc_out_d.ap().opt()]).then_inc(sem, 1),
        reads=[pref + "_ccin"], writes=[], inc=0)
    p.op("gpsimd", lambda g: g.wait_ge(sem, 1),
         reads=[], writes=[pref + "_ccdone"], inc=1)
    p.op("gpsimd", lambda g: g.dma_start(out=dst_sb[:], in_=cc_out_d[:]),
         reads=[pref + "_ccdone"], writes=[dst_key], inc=16)


def band_mask_tiles():
    out = np.zeros((6, 128, 512), dtype=np.float32)
    k = np.arange(128)[:, None]
    q = np.arange(512)[None, :]
    for i, d0 in enumerate([-128, 0, 128, 256, 384, 512]):
        out[i] = (np.abs(d0 + k - q) <= 32).astype(np.float32)
    return np.ascontiguousarray(np.transpose(out, (1, 0, 2)).reshape(128, 6 * 512))


def build_L3_attn(p, a, io, x_res, scratch):
    nc = p.nc
    xn16 = scratch["xn16"]
    psb, psk = scratch["psb"], scratch["psk"]
    ident = scratch["ident_sb"]
    ones_row = scratch["ones_row"]
    zeros_row = scratch["zeros_row"]
    H, DKH = 8, 128

    xn32 = nc.dram_tensor("l3_xn32", [T, D], F32, kind="Internal")
    inv16d = nc.dram_tensor("l3_inv16", [T, D], BF16, kind="Internal")
    xqk16d = nc.dram_tensor("l3_xqk16", [T, D], BF16, kind="Internal")
    q16d = nc.dram_tensor("l3_q16", [T, D], BF16, kind="Internal")
    k16d = nc.dram_tensor("l3_k16", [T, D], BF16, kind="Internal")
    v16d = nc.dram_tensor("l3_v16", [T, D], BF16, kind="Internal")
    qn16d = nc.dram_tensor("l3_qn16", [T, D], BF16, kind="Internal")
    kn16d = nc.dram_tensor("l3_kn16", [T, D], BF16, kind="Internal")
    qs16d = nc.dram_tensor("l3_qs16", [T, D], BF16, kind="Internal")
    ks16d = nc.dram_tensor("l3_ks16", [T, D], BF16, kind="Internal")
    stage_d = scratch["dram_bf16_DT"]
    ctxd = nc.dram_tensor("l3_ctx16", [D, T], BF16, kind="Internal")
    cc1i = nc.dram_tensor("l3_cc1i", [1, 8], F32, kind="Internal")
    cc1o = nc.dram_tensor("l3_cc1o", [1, 8], F32, kind="Internal")
    cc2i = nc.dram_tensor("l3_cc2i", [1, 8], F32, kind="Internal")
    cc2o = nc.dram_tensor("l3_cc2o", [1, 8], F32, kind="Internal")
    bmask_d = nc.inline_tensor(band_mask_tiles(), "l3_bmask")

    cc_sem1 = a.stack[0].enter_context(nc.semaphore("l3_ccsem1"))
    cc_sem2 = a.stack[0].enter_context(nc.semaphore("l3_ccsem2"))

    with a.scope():
        ln_pass(p, a, x_res, xn16, "l3ln", dst32=xn32)

    with a.scope():
        gacc = a.sb("l3_gacc", [128, 1], F32)
        p.op("vector", lambda v: v.memset(gacc[:], 0.0), writes=["l3_gacc"])
        garow = a.sb("l3_garow", [1, 8], F32)
        gres = a.sb("l3_gres", [1, 8], F32)
        with a.scope():
            wp = load_weight_T_bf16(p, a, io["L3_attn_Wproj"], "L3_attn_Wproj", "l3_wp", stage_d, "l3_stgP")
            bp = load_row(p, a, io["L3_attn_bproj"], "l3_bp")
            xf = [a.sb(f"l3_xf{kc}", [128, T], BF16) for kc in range(DC)]
            xfk = [f"l3_xf{kc}" for kc in range(DC)]
            fm_loads(p, xf, xfk, xn16)
            xt32 = a.sb("l3_xt32", [128, D], F32)
            res_b = a.sb("l3_resb", [128, D], F32)
            sq_b = a.sb("l3_sqb", [128, D], F32)
            nrm = a.sb("l3_nrm", [128, 4], F32)
            inv16t = a.sb("l3_inv16t", [128, 512], BF16)

            def epi_inv(p_, ps, pskey, tt_abs, ns_abs, nlen):
                p_.op("vector", lambda v: v.tensor_copy(inv16t[:, 0:nlen], ps[0:128, 0:nlen]),
                      reads=[pskey], writes=["l3_inv16t"])
                p_.op("sync", lambda s: s.dma_start(out=inv16d[tt_abs:tt_abs + 128, ns_abs:ns_abs + nlen], in_=inv16t[:, 0:nlen]),
                      reads=["l3_inv16t"], writes=["l3inv16d"], inc=16)
                if ns_abs == 0:
                    p_.op("sync", lambda s: s.dma_start(out=xt32[:], in_=xn32[tt_abs:tt_abs + 128, :]),
                          reads=[xn32.name], writes=["l3_xt32"], inc=16)
                p_.op("vector", lambda v: v.tensor_sub(res_b[:, ns_abs:ns_abs + nlen], xt32[:, ns_abs:ns_abs + nlen], ps[0:128, 0:nlen]),
                      reads=[pskey, "l3_xt32"], writes=["l3_resb"])
                if ns_abs + nlen == D:
                    p_.op("scalar", lambda s: s.activation(sq_b[:], res_b[:], AF.Square, accum_out=nrm[:, 0:1]),
                          reads=["l3_resb"], writes=["l3_sqb", "l3_nrm0"])
                    p_.op("scalar", lambda s: s.activation(sq_b[:], xt32[:], AF.Square, accum_out=nrm[:, 1:2]),
                          reads=["l3_xt32"], writes=["l3_sqb", "l3_nrm1"])
                    p_.op("scalar", lambda s: s.activation(nrm[:, 0:1], nrm[:, 0:1], AF.Sqrt),
                          reads=["l3_nrm0"], writes=["l3_nrm0"])
                    p_.op("scalar", lambda s: s.activation(nrm[:, 1:2], nrm[:, 1:2], AF.Sqrt),
                          reads=["l3_nrm1"], writes=["l3_nrm1"])
                    p_.op("vector", lambda v: v.tensor_scalar(nrm[:, 1:2], nrm[:, 1:2], 1e-8, None, ALU.add),
                          reads=["l3_nrm1"], writes=["l3_nrm1"])
                    p_.op("vector", lambda v: v.reciprocal(nrm[:, 2:3], nrm[:, 1:2]),
                          reads=["l3_nrm1"], writes=["l3_nrm2"])
                    p_.op("vector", lambda v: v.tensor_mul(nrm[:, 3:4], nrm[:, 0:1], nrm[:, 2:3]),
                          reads=["l3_nrm0", "l3_nrm2"], writes=["l3_nrm3"])
                    p_.op("vector", lambda v: v.tensor_add(gacc[:], gacc[:], nrm[:, 3:4]),
                          reads=["l3_nrm3", "l3_gacc"], writes=["l3_gacc"])

            proj_tok(p, xf, xfk, wp, "l3_wp", D, bp, "l3_bp", ones_row, "ones_row",
                     psb[0:2], psk[0:2], epi_inv)
            p.op("vector", lambda v: v.memset(garow[:], 0.0), writes=["l3_garow"])
            onescol0, onescol0k = p.cst(1.0)
            p.op("tensor", lambda t: t.matmul(psb[5][0:1, 0:1], gacc[:], onescol0, start=True, stop=True),
                 reads=["l3_gacc", onescol0k], writes=[psk[5]], inc=1)
            p.op("scalar", lambda s: s.copy(garow[0:1, 0:1], psb[5][0:1, 0:1]),
                 reads=[psk[5]], writes=["l3_garow"])

        allreduce_scalar(p, a, "l3g", cc1i, cc1o, garow, "l3_garow", gres, "l3_gres", cc_sem1)
        g01 = a.sb("l3_g01", [1, 1], F32)
        p.op("vector", lambda v: v.tensor_scalar(g01[:], gres[0:1, 0:1], 1.0 / (8 * T), None, ALU.mult),
             reads=["l3_gres"], writes=["l3_g01"])
        p.op("vector", lambda v: v.tensor_scalar(g01[:], g01[:], 0.3, None, ALU.is_gt),
             reads=["l3_g01"], writes=["l3_g01"])
        g01c = a.sb("l3_g01c", [128, 1], F32)
        p.op("tensor", lambda t: t.matmul(psb[5][0:128, 0:1], ones_row[0:1, 0:128], g01[:], start=True, stop=True),
             reads=["l3_g01", "ones_row"], writes=[psk[5]], inc=1)
        p.op("scalar", lambda s: s.copy(g01c[:], psb[5][0:128, 0:1]),
             reads=[psk[5]], writes=["l3_g01c"])

        with a.scope():
            t16a = a.sb("l3_t16a", [128, D], BF16)
            t16b = a.sb("l3_t16b", [128, D], BF16)
            t16c = a.sb("l3_t16c", [128, D], BF16)
            for tt in range(NT):
                p.op("sync", lambda s, tt=tt: s.dma_start(out=t16a[:], in_=xn16[tt * 128:(tt + 1) * 128, :]),
                     reads=[xn16.name], writes=["l3_t16a"], inc=16)
                p.op("sync", lambda s, tt=tt: s.dma_start(out=t16b[:], in_=inv16d[tt * 128:(tt + 1) * 128, :]),
                     reads=["l3inv16d"], writes=["l3_t16b"], inc=16)
                p.op("vector", lambda v: v.tensor_sub(t16c[:], t16b[:], t16a[:]),
                     reads=["l3_t16a", "l3_t16b"], writes=["l3_t16c"])
                p.op("vector", lambda v: v.scalar_tensor_tensor(t16c[:], t16c[:], g01c[:], t16a[:], ALU.mult, ALU.add),
                     reads=["l3_t16c", "l3_g01c", "l3_t16a"], writes=["l3_t16c"])
                p.op("sync", lambda s, tt=tt: s.dma_start(out=xqk16d[tt * 128:(tt + 1) * 128, :], in_=t16c[:]),
                     reads=["l3_t16c"], writes=["l3xqk16d"], inc=16)

        with a.scope():
            wq = load_weight_T_bf16(p, a, io["L3_attn_Wq"], "L3_attn_Wq", "l3_wq", stage_d, "l3_stgQ")
            wk = load_weight_T_bf16(p, a, io["L3_attn_Wk"], "L3_attn_Wk", "l3_wk", stage_d, "l3_stgK")
            wv = load_weight_T_bf16(p, a, io["L3_attn_Wv"], "L3_attn_Wv", "l3_wv", stage_d, "l3_stgV")
            bq = load_row(p, a, io["L3_attn_bq"], "l3_bq")
            bk_ = load_row(p, a, io["L3_attn_bk"], "l3_bk")
            bv = load_row(p, a, io["L3_attn_bv"], "l3_bv")
            xq = [a.sb(f"l3_xq{kc}", [128, T], BF16) for kc in range(DC)]
            xqk_ = [f"l3_xq{kc}" for kc in range(DC)]
            fm_loads(p, xq, xqk_, xqk16d)
            ot = a.sb("l3_ot", [128, 512], BF16)

            def mk_epi(dst, dk_):
                def epi(p_, ps, pskey, tt_abs, ns_abs, nlen):
                    p_.op("vector", lambda v: v.tensor_copy(ot[:, 0:nlen], ps[0:128, 0:nlen]),
                          reads=[pskey], writes=["l3_ot"])
                    p_.op("sync", lambda s: s.dma_start(out=dst[tt_abs:tt_abs + 128, ns_abs:ns_abs + nlen], in_=ot[:, 0:nlen]),
                          reads=["l3_ot"], writes=[dk_], inc=16)
                return epi

            proj_tok(p, xq, xqk_, wq, "l3_wq", D, bq, "l3_bq", ones_row, "ones_row",
                     psb[0:2], psk[0:2], mk_epi(q16d, "l3q16d"))
            proj_tok(p, xq, xqk_, wk, "l3_wk", D, bk_, "l3_bk", ones_row, "ones_row",
                     psb[2:4], psk[2:4], mk_epi(k16d, "l3k16d"))
            xv = [a.sb(f"l3_xv{kc}", [128, T], BF16) for kc in range(DC)]
            xvk = [f"l3_xv{kc}" for kc in range(DC)]
            fm_loads(p, xv, xvk, xn16)
            proj_tok(p, xv, xvk, wv, "l3_wv", D, bv, "l3_bv", ones_row, "ones_row",
                     psb[4:6], psk[4:6], mk_epi(v16d, "l3v16d"))

        with a.scope():
            qt16 = a.sb("l3_qt16", [128, D], BF16)
            on16 = a.sb("l3_on16", [128, D], BF16)
            os16 = a.sb("l3_os16", [128, D], BF16)
            sqd = a.sb("l3_sqd", [128, 128], F32)
            hstat = a.sb("l3_hst", [128, 4], F32)
            qsum = a.sb("l3_qsum", [1, 2 * D], F32)
            onescol, onescolk = p.cst(1.0)
            e12, e12k = p.cst(1e-12)
            for wi, (srck, nd, ndk, sd, sdk) in enumerate((
                    ("l3q16d", qn16d, "l3qn16d", qs16d, "l3qs16d"),
                    ("l3k16d", kn16d, "l3kn16d", ks16d, "l3ks16d"))):
                srcd = q16d if wi == 0 else k16d
                for tt in range(NT):
                    p.op("sync", lambda s, tt=tt, srcd=srcd: s.dma_start(out=qt16[:], in_=srcd[tt * 128:(tt + 1) * 128, :]),
                         reads=[srck], writes=["l3_qt16"], inc=16)
                    oc16 = scratch["ones_col16"]
                    for half in range(2):
                        p.op("tensor", lambda t, half=half, tt=tt, oc16=oc16: t.matmul(
                            psb[6 + half][0:1, 0:512], oc16[:],
                            qt16[:, half * 512:(half + 1) * 512],
                            start=(tt == 0), stop=(tt == NT - 1)),
                            reads=["l3_qt16", "ones_col16"] + ([psk[6 + half]] if tt == 0 else []),
                            writes=([psk[6 + half]] if tt in (0, NT - 1) else []),
                            inc=1)
                    for h in range(H):
                        sl = qt16[:, h * 128:(h + 1) * 128]
                        p.op("scalar", lambda s, sl=sl: s.activation(sqd[:], sl, AF.Square, accum_out=hstat[:, 0:1]),
                             reads=["l3_qt16"], writes=["l3_sqd", "l3_hst0"])
                        p.op("scalar", lambda s: s.activation(hstat[:, 1:2], hstat[:, 0:1], AF.Sqrt),
                             reads=["l3_hst0"], writes=["l3_hst1"])
                        p.op("vector", lambda v: v.tensor_scalar(hstat[:, 1:2], hstat[:, 1:2], e12, None, ALU.max),
                             reads=["l3_hst1", e12k], writes=["l3_hst1"])
                        p.op("vector", lambda v: v.reciprocal(hstat[:, 2:3], hstat[:, 1:2]),
                             reads=["l3_hst1"], writes=["l3_hst2"])
                        p.op("vector", lambda v, sl=sl, h=h: v.tensor_scalar(on16[:, h * 128:(h + 1) * 128], sl, hstat[:, 2:3], None, ALU.mult),
                             reads=["l3_qt16", "l3_hst2"], writes=["l3_on16"])
                        p.op("vector", lambda v, sl=sl: v.reduce_max(hstat[:, 0:1], sl, axis=AX.X, negate=True),
                             reads=["l3_qt16"], writes=["l3_hst0"])
                        p.op("scalar", lambda s, sl=sl: s.activation(sqd[:], sl, AF.Exp, bias=hstat[:, 0:1], accum_out=hstat[:, 1:2]),
                             reads=["l3_qt16", "l3_hst0"], writes=["l3_sqd", "l3_hst1"])
                        p.op("vector", lambda v: v.reciprocal(hstat[:, 2:3], hstat[:, 1:2]),
                             reads=["l3_hst1"], writes=["l3_hst2"])
                        p.op("vector", lambda v, h=h: v.tensor_scalar(os16[:, h * 128:(h + 1) * 128], sqd[:], hstat[:, 2:3], None, ALU.mult),
                             reads=["l3_sqd", "l3_hst2"], writes=["l3_os16"])
                    p.op("sync", lambda s, tt=tt, nd=nd: s.dma_start(out=nd[tt * 128:(tt + 1) * 128, :], in_=on16[:]),
                         reads=["l3_on16"], writes=[ndk], inc=16)
                    p.op("sync", lambda s, tt=tt, sd=sd: s.dma_start(out=sd[tt * 128:(tt + 1) * 128, :], in_=os16[:]),
                         reads=["l3_os16"], writes=[sdk], inc=16)
                for half in range(2):
                    p.op("scalar", lambda s, half=half, wi=wi: s.copy(qsum[0:1, wi * D + half * 512: wi * D + (half + 1) * 512], psb[6 + half][0:1, 0:512]),
                         reads=[psk[6 + half]], writes=["l3_qsum"])

        with a.scope():
            hqk = a.sb("l3_hqk", [1, 16], F32)
            hj = a.sb("l3_hj", [1, 8], F32)
            r1 = a.sb("l3_r1", [1, 128], F32)
            r2 = a.sb("l3_r2", [1, 128], F32)
            rs = a.sb("l3_rs", [1, 4], F32)
            e10, e10k = p.cst(1e-10)
            onescol, onescolk = p.cst(1.0)
            for wi in range(2):
                for h in range(H):
                    sl = qsum[0:1, wi * D + h * 128: wi * D + (h + 1) * 128]
                    p.op("vector", lambda v, sl=sl: v.reduce_max(rs[0:1, 0:1], sl, axis=AX.X, negate=True),
                         reads=["l3_qsum"], writes=["l3_rs0"])
                    p.op("scalar", lambda s, sl=sl: s.activation(r1[:], sl, AF.Exp, bias=rs[0:1, 0:1], accum_out=rs[0:1, 1:2]),
                         reads=["l3_qsum", "l3_rs0"], writes=["l3_r1", "l3_rs1"])
                    p.op("vector", lambda v: v.reciprocal(rs[0:1, 2:3], rs[0:1, 1:2]),
                         reads=["l3_rs1"], writes=["l3_rs2"])
                    p.op("vector", lambda v: v.tensor_scalar(r1[:], r1[:], rs[0:1, 2:3], None, ALU.mult),
                         reads=["l3_r1", "l3_rs2"], writes=["l3_r1"])
                    p.op("scalar", lambda s: s.activation(r2[:], r1[:], AF.Ln, bias=e10[0:1, 0:1]),
                         reads=["l3_r1", e10k], writes=["l3_r2"])
                    p.op("vector", lambda v, wi=wi, h=h: v.scalar_tensor_tensor(r2[:], r1[:], 1.0, r2[:], ALU.mult, ALU.mult, accum_out=hqk[0:1, wi * 8 + h: wi * 8 + h + 1]),
                         reads=["l3_r1", "l3_r2"], writes=["l3_r2", "l3_hqk"])
            qsfm = a.sb("l3_qsfm", [128, T], BF16)
            ksfm = a.sb("l3_ksfm", [128, T], BF16)
            uln = a.sb("l3_uln", [128, 512], F32)
            ujnk = a.sb("l3_ujnk", [128, 512], F32)
            hpart = a.sb("l3_hpart", [128, 2], F32)
            for h in range(H):
                p.op("sync", lambda s, h=h: s.dma_start(out=qsfm[:], in_=qs16d[:, h * 128:(h + 1) * 128], transpose=True),
                     reads=["l3qs16d"], writes=["l3_qsfm"], inc=16)
                p.op("sync", lambda s, h=h: s.dma_start(out=ksfm[:], in_=ks16d[:, h * 128:(h + 1) * 128], transpose=True),
                     reads=["l3ks16d"], writes=["l3_ksfm"], inc=16)
                p.op("vector", lambda v: v.memset(hpart[:, 0:1], 0.0), writes=["l3_hp0"])
                for qt in range(NT):
                    for kc in range(TT512):
                        b, bk = psb[(qt * TT512 + kc) % 2], psk[(qt * TT512 + kc) % 2]
                        p.op("tensor", lambda t, qt=qt, kc=kc, b=b: t.matmul(
                            b[0:128, 0:512], qsfm[:, qt * 128:(qt + 1) * 128],
                            ksfm[:, kc * 512:(kc + 1) * 512], start=True, stop=True),
                            reads=["l3_qsfm", "l3_ksfm"], writes=[bk], inc=1)
                        p.op("scalar", lambda s, b=b: s.activation(uln[:], b[0:128, 0:512], AF.Ln, bias=e10),
                             reads=[bk, e10k], writes=["l3_uln"])
                        p.op("vector", lambda v, b=b: v.scalar_tensor_tensor(ujnk[:], uln[:], 1.0, b[0:128, 0:512], ALU.mult, ALU.mult, accum_out=hpart[:, 1:2]),
                             reads=["l3_uln", bk], writes=["l3_ujnk", "l3_hp1"])
                        p.op("vector", lambda v: v.tensor_add(hpart[:, 0:1], hpart[:, 0:1], hpart[:, 1:2]),
                             reads=["l3_hp0", "l3_hp1"], writes=["l3_hp0"])
                p.op("tensor", lambda t: t.matmul(psb[5][0:1, 0:1], hpart[:, 0:1], onescol, start=True, stop=True),
                     reads=["l3_hp0", onescolk], writes=[psk[5]], inc=1)
                p.op("scalar", lambda s, h=h: s.copy(hj[0:1, h:h + 1], psb[5][0:1, 0:1]),
                     reads=[psk[5]], writes=["l3_hj"])
            gate = a.sb("l3_gate", [1, 1], F32)
            t8 = a.sb("l3_t8", [1, 8], F32)
            p.op("vector", lambda v: v.tensor_add(t8[:], hqk[0:1, 0:8], hqk[0:1, 8:16]),
                 reads=["l3_hqk"], writes=["l3_t8"])
            p.op("vector", lambda v: v.tensor_sub(t8[:], t8[:], hj[:]),
                 reads=["l3_t8", "l3_hj"], writes=["l3_t8"])
            p.op("vector", lambda v: v.reduce_sum(gate[:], t8[:], axis=AX.X),
                 reads=["l3_t8"], writes=["l3_gate"])
            p.op("vector", lambda v: v.tensor_scalar(gate[:], gate[:], 1.0 / 8, None, ALU.mult),
                 reads=["l3_gate"], writes=["l3_gate"])
            p.op("vector", lambda v: v.tensor_scalar(gate[:], gate[:], 0.15, None, ALU.is_gt),
                 reads=["l3_gate"], writes=["l3_gate"])
            gate_keep = a.sb("l3_gatek", [1, 1], F32)
            p.op("vector", lambda v: v.tensor_copy(gate_keep[:], gate[:]),
                 reads=["l3_gate"], writes=["l3_gatek"])

        with a.scope():
            knfm = [a.sb(f"l3_knfm{c}", [128, T], BF16) for c in range(DC)]
            for c in range(DC):
                p.op("sync", lambda s, c=c: s.dma_start(out=knfm[c][:], in_=kn16d[:, c * 128:(c + 1) * 128], transpose=True),
                     reads=["l3kn16d"], writes=[f"l3_knfm{c}"], inc=16)
            qnfm = a.sb("l3_qnfm", [128, 512], BF16)
            v_aug = a.sb("l3_vaug", [128, NT * 129], BF16)
            bmask = a.sb("l3_bm", [128, 6 * 512], F32)
            p.op("sync", lambda s: s.dma_start(out=bmask[:], in_=bmask_d[:]),
                 reads=[], writes=["l3_bm"], inc=16)
            PT = [a.sb(f"l3_pt{i}", [128, 512], BF16) for i in range(2)]
            ub = a.sb("l3_ub", [128, 512], F32)
            srow = a.sb("l3_srow", [1, 512], F32)
            ctx_f = a.sb("l3_ctxf", [128, 512], F32)
            ctx16 = a.sb("l3_c16", [128, 512], BF16)
            onescol, onescolk = p.cst(1.0)
            e110, e110k = p.cst(1e-10)
            for h in range(H):
                vv = v16d.rearrange("(tt p) n -> p tt n", p=128)
                p.op("sync", lambda s, h=h, vv=vv: s.dma_start(
                    out=v_aug[:].rearrange("p (tt c) -> p tt c", c=129)[:, :, 0:128],
                    in_=vv[:, :, h * 128:(h + 1) * 128]),
                    reads=["l3v16d"], writes=["l3_vaug"], inc=16)
                p.op("vector", lambda v: v.memset(v_aug[:].rearrange("p (tt c) -> p tt c", c=129)[:, :, 128:129], 1.0),
                     reads=[], writes=["l3_vaug"])
                for qs in range(TT512):
                    p.op("sync", lambda s, h=h, qs=qs: s.dma_start(
                        out=qnfm[:], in_=qn16d[qs * 512:(qs + 1) * 512, h * 128:(h + 1) * 128], transpose=True),
                        reads=["l3qn16d"], writes=["l3_qnfm"], inc=16)
                    kclo = max(0, 4 * qs - 1)
                    kchi = min(NT - 1, 4 * qs + 4)
                    kcs = list(range(kclo, kchi + 1))
                    for i, kc in enumerate(kcs):
                        b, bk = psb[i % 2], psk[i % 2]
                        pt, ptk = PT[i % 2], f"l3_pt{i % 2}"
                        p.op("tensor", lambda t, kc=kc, b=b, h=h: t.matmul(
                            b[0:128, 0:512], knfm[h][:, kc * 128:(kc + 1) * 128],
                            qnfm[:], start=True, stop=True),
                            reads=[f"l3_knfm{h}", "l3_qnfm"], writes=[bk], inc=1)
                        p.op("vector", lambda v, b=b: v.tensor_scalar(ub[:], b[0:128, 0:512], -2.0, 2.0, ALU.mult, ALU.add),
                             reads=[bk], writes=["l3_ub"])
                        p.op("vector", lambda v: v.tensor_scalar_min(ub[:], ub[:], 4.0),
                             reads=["l3_ub"], writes=["l3_ub"])
                        p.op("vector", lambda v: v.tensor_scalar_max(ub[:], ub[:], 0.0),
                             reads=["l3_ub"], writes=["l3_ub"])
                        p.op("scalar", lambda s: s.activation(ub[:], ub[:], AF.Sqrt),
                             reads=["l3_ub"], writes=["l3_ub"])
                        mneg, mnegk = p.cst(-1.0)
                        p.op("scalar", lambda s, mneg=mneg: s.activation(ub[:], ub[:], AF.Exp, scale=mneg),
                             reads=["l3_ub", mnegk], writes=["l3_ub"])
                        di = kc * 128 - qs * 512
                        bi = (di + 128) // 128
                        p.op("vector", lambda v, pt=pt, bi=bi: v.tensor_mul(pt[:], ub[:], bmask[:, bi * 512:(bi + 1) * 512]),
                             reads=["l3_ub", "l3_bm"], writes=[ptk])
                        p.op("tensor", lambda t, kc=kc, pt=pt, i=i, n=len(kcs): t.matmul(
                            psb[4][0:128, 0:512], v_aug[:, kc * 129: kc * 129 + 128],
                            pt[:], start=(i == 0), stop=False),
                            reads=["l3_vaug", ptk] + ([psk[4]] if i == 0 else []),
                            writes=([psk[4]] if i == 0 else []), inc=0)
                        p.op("tensor", lambda t, kc=kc, pt=pt, i=i, n=len(kcs): t.matmul(
                            psb[6][0:1, 0:512], v_aug[:, kc * 129 + 128: kc * 129 + 129],
                            pt[:], start=(i == 0), stop=(i == n - 1)),
                            reads=["l3_vaug", ptk] + ([psk[6]] if i == 0 else []),
                            writes=([psk[6]] if i in (0, len(kcs) - 1) else []),
                            inc=(1 if i == len(kcs) - 1 else 0))
                    p.op("scalar", lambda s: s.copy(srow[:], psb[6][0:1, 0:512]),
                         reads=[psk[6]], writes=["l3_srow"])
                    p.op("vector", lambda v: v.tensor_scalar(srow[:], srow[:], gate_keep[0:1, 0:1], e110[0:1, 0:1], ALU.mult, ALU.add),
                         reads=["l3_srow", "l3_gatek", e110k], writes=["l3_srow"])
                    p.op("vector", lambda v: v.reciprocal(srow[:], srow[:]),
                         reads=["l3_srow"], writes=["l3_srow"])
                    p.op("vector", lambda v: v.tensor_scalar(srow[:], srow[:], gate_keep[0:1, 0:1], None, ALU.mult),
                         reads=["l3_srow", "l3_gatek"], writes=["l3_srow"])
                    p.op("scalar", lambda s: s.copy(ctx_f[:], psb[4][0:128, 0:512]),
                         reads=[psk[4]], writes=["l3_ctxf"])
                    p.op("tensor", lambda t: t.matmul(psb[5][0:128, 0:512], ones_row[0:1, 0:128], srow[:], start=True, stop=True),
                         reads=["l3_srow", "ones_row"], writes=[psk[5]], inc=1)
                    p.op("vector", lambda v: v.tensor_mul(ctx16[:], ctx_f[:], psb[5][0:128, 0:512]),
                         reads=["l3_ctxf", psk[5]], writes=["l3_c16"])
                    p.op("sync", lambda s, h=h, qs=qs: s.dma_start(out=ctxd[h * 128:(h + 1) * 128, qs * 512:(qs + 1) * 512], in_=ctx16[:]),
                         reads=["l3_c16"], writes=["l3ctxd"], inc=16)

        with a.scope():
            cf = [a.sb(f"l3_cf{c}", [128, T], BF16) for c in range(DC)]
            cfk = [f"l3_cf{c}" for c in range(DC)]
            for c in range(DC):
                p.op("sync", lambda s, c=c: s.dma_start(out=cf[c][:], in_=ctxd[c * 128:(c + 1) * 128, :]),
                     reads=["l3ctxd"], writes=[cfk[c]], inc=16)
            w1t = load_weight_T_bf16(p, a, io["L3_attn_W1"], "L3_attn_W1", "l3_w1t", stage_d, "l3_stg1")
            b1r = load_row(p, a, io["L3_attn_b1"], "l3_b1r")
            w2r = load_row(p, a, io["L3_attn_W2"], "l3_w2r")
            b2r = load_row(p, a, io["L3_attn_b2"], "l3_b2r")
            w2b = a.sb("l3_w2b", [128, 512], F32)
            p.op("tensor", lambda t: t.matmul(psb[5][0:128, 0:512], ones_row[0:1, 0:128], w2r[:], start=True, stop=True),
                 reads=["l3_w2r", "ones_row"], writes=[psk[5]], inc=1)
            p.op("vector", lambda v: v.tensor_copy(w2b[:], psb[5][0:128, 0:512]),
                 reads=[psk[5]], writes=["l3_w2b"])
            b2c = a.sb("l3_b2c", [128, 1], F32)
            p.op("tensor", lambda t: t.matmul(psb[5][0:128, 0:1], ones_row[0:1, 0:128], b2r[0:1, 0:1], start=True, stop=True),
                 reads=["l3_b2r", "ones_row"], writes=[psk[5]], inc=1)
            p.op("scalar", lambda s: s.copy(b2c[:], psb[5][0:128, 0:1]),
                 reads=[psk[5]], writes=["l3_b2c"])
            hrelu = a.sb("l3_hrelu", [128, 512], F32)
            junk = a.sb("l3_junk", [128, 512], F32)
            cstat = a.sb("l3_cst", [128, 2], F32)
            socacc = a.sb("l3_soc", [128, 1], F32)
            p.op("vector", lambda v: v.memset(socacc[:], 0.0), writes=["l3_soc"])

            def epi_h1(p_, ps, pskey, tt_abs, ns_abs, nlen):
                p_.op("vector", lambda v: v.tensor_relu(hrelu[:, 0:nlen], ps[0:128, 0:nlen]),
                      reads=[pskey], writes=["l3_hrelu"])
                p_.op("vector", lambda v: v.scalar_tensor_tensor(junk[:, 0:nlen], hrelu[:, 0:nlen], 1.0, w2b[:, 0:nlen], ALU.mult, ALU.mult, accum_out=cstat[:, 0:1]),
                      reads=["l3_hrelu", "l3_w2b"], writes=["l3_junk", "l3_cst0"])
                p_.op("scalar", lambda s: s.activation(cstat[:, 1:2], cstat[:, 0:1], AF.Sigmoid, bias=b2c[:]),
                      reads=["l3_cst0", "l3_b2c"], writes=["l3_cst1"])
                p_.op("vector", lambda v: v.tensor_add(socacc[:], socacc[:], cstat[:, 1:2]),
                      reads=["l3_soc", "l3_cst1"], writes=["l3_soc"])

            proj_tok(p, cf, cfk, w1t, "l3_w1t", 512, b1r, "l3_b1r", ones_row, "ones_row",
                     psb[0:2], psk[0:2], epi_h1)
            onescol, onescolk = p.cst(1.0)
            socrow = a.sb("l3_socrow", [1, 8], F32)
            socres = a.sb("l3_socres", [1, 8], F32)
            p.op("vector", lambda v: v.memset(socrow[:], 0.0), writes=["l3_socrow"])
            p.op("tensor", lambda t: t.matmul(psb[5][0:1, 0:1], socacc[:], onescol, start=True, stop=True),
                 reads=["l3_soc", onescolk], writes=[psk[5]], inc=1)
            p.op("scalar", lambda s: s.copy(socrow[0:1, 0:1], psb[5][0:1, 0:1]),
                 reads=[psk[5]], writes=["l3_socrow"])
            allreduce_scalar(p, a, "l3s", cc2i, cc2o, socrow, "l3_socrow", socres, "l3_socres", cc_sem2)
            csc = a.sb("l3_csc", [1, 1], F32)
            m2c, m2k = p.cst(-2.0)
            p.op("vector", lambda v: v.tensor_scalar(csc[:], socres[0:1, 0:1], 1.0 / (8 * T), None, ALU.mult),
                 reads=["l3_socres"], writes=["l3_csc"])
            p.op("scalar", lambda s: s.activation(csc[:], csc[:], AF.Exp, scale=m2c[0:1, 0:1]),
                 reads=["l3_csc", m2k], writes=["l3_csc"])
            p.op("vector", lambda v: v.tensor_scalar_min(csc[:], csc[:], 1.5),
                 reads=["l3_csc"], writes=["l3_csc"])
            p.op("vector", lambda v: v.tensor_scalar_max(csc[:], csc[:], 0.5),
                 reads=["l3_csc"], writes=["l3_csc"])
            p.op("vector", lambda v: v.tensor_scalar(csc[:], csc[:], 0.3, 0.7, ALU.mult, ALU.add),
                 reads=["l3_csc"], writes=["l3_csc"])
            ccol = a.sb("l3_ccol", [128, 1], F32)
            p.op("tensor", lambda t: t.matmul(psb[5][0:128, 0:1], ones_row[0:1, 0:128], csc[:], start=True, stop=True),
                 reads=["l3_csc", "ones_row"], writes=[psk[5]], inc=1)
            p.op("scalar", lambda s: s.copy(ccol[:], psb[5][0:128, 0:1]),
                 reads=[psk[5]], writes=["l3_ccol"])
            for c in range(DC):
                p.op("vector", lambda v, c=c: v.tensor_scalar(cf[c][:], cf[c][:], ccol[:], None, ALU.mult),
                     reads=[cfk[c], "l3_ccol"], writes=[cfk[c]])
            wo = load_weight_T_bf16(p, a, io["L3_attn_Wo"], "L3_attn_Wo", "l3_wo", stage_d, "l3_stgO")
            bo = load_row(p, a, io["L3_attn_bo"], "l3_bo")
            proj_tok(p, cf, cfk, wo, "l3_wo", D, bo, "l3_bo", ones_row, "ones_row",
                     psb[0:4], psk[0:4], accum_epilogue_tok(p, scratch, x_res))


def build_L3_ffn(p, a, io, x_res, scratch):
    nc = p.nc
    xn16 = scratch["xn16"]
    psb, psk = scratch["psb"], scratch["psk"]
    ones_row = scratch["ones_row"]
    w_d = io["L3_ffn_weight"]
    wt_d = nc.dram_tensor("l3f_wt16", [D, D], BF16, kind="Internal")
    scratch["_l3f_wt"] = wt_d
    dbgrow = a.sb("l3f_dbgrow", [1, 8], F32)
    scratch["_l3f_dbg"] = dbgrow

    with a.scope():
        ln_pass(p, a, x_res, xn16, "l3fln")

    with a.scope():
        wtile = a.sb("l3f_wtile", [128, D], F32)
        absacc = a.sb("l3f_aa", [128, 2], F32)
        p.op("vector", lambda v: v.memset(absacc[:], 0.0), writes=["l3f_aa"])
        for rc in range(DC):
            p.op("sync", lambda s, rc=rc: s.dma_start(out=wtile[:], in_=w_d[rc * 128:(rc + 1) * 128, :]),
                 reads=[w_d.name], writes=["l3f_wtile"], inc=16)
            p.op("vector", lambda v: v.reduce_sum(absacc[:, 1:2], wtile[:], axis=AX.X, apply_absolute_value=True),
                 reads=["l3f_wtile"], writes=["l3f_aa1"])
            p.op("vector", lambda v: v.tensor_add(absacc[:, 0:1], absacc[:, 0:1], absacc[:, 1:2]),
                 reads=["l3f_aa", "l3f_aa1"], writes=["l3f_aa"])
        onescol, onescolk = p.cst(1.0)
        drow = a.sb("l3f_drow", [1, 1], F32)
        p.op("tensor", lambda t: t.matmul(psb[5][0:1, 0:1], absacc[:, 0:1], onescol, start=True, stop=True),
             reads=["l3f_aa", onescolk], writes=[psk[5]], inc=1)
        p.op("scalar", lambda s: s.copy(drow[:], psb[5][0:1, 0:1]),
             reads=[psk[5]], writes=["l3f_drow"])
        p.op("vector", lambda v: v.tensor_scalar(drow[:], drow[:], 0.7 / (D * D), None, ALU.mult),
             reads=["l3f_drow"], writes=["l3f_drow"])
        p.op("vector", lambda v: v.tensor_copy(dbgrow[0:1, 0:1], drow[:]),
             reads=["l3f_drow"], writes=["l3f_dbgrow"])
        dcol = a.sb("l3f_dcol", [128, 2], F32)
        p.op("tensor", lambda t: t.matmul(psb[5][0:128, 0:1], ones_row[0:1, 0:128], drow[:], start=True, stop=True),
             reads=["l3f_drow", "ones_row"], writes=[psk[5]], inc=1)
        p.op("scalar", lambda s: s.copy(dcol[:, 0:1], psb[5][0:128, 0:1]),
             reads=[psk[5]], writes=["l3f_dcol"])
        p.op("vector", lambda v: v.tensor_scalar(dcol[:, 1:2], dcol[:, 0:1], -1.0, None, ALU.mult),
             reads=["l3f_dcol"], writes=["l3f_dcol"])
        al_r = load_row(p, a, io["L3_ffn_alpha"], "l3f_al")
        acol = a.sb("l3f_acol", [128, 1], F32)
        p.op("tensor", lambda t: t.matmul(psb[5][0:128, 0:1], ones_row[0:1, 0:128], al_r[0:1, 0:1], start=True, stop=True),
             reads=["l3f_al", "ones_row"], writes=[psk[5]], inc=1)
        p.op("scalar", lambda s: s.copy(acol[:], psb[5][0:128, 0:1]),
             reads=[psk[5]], writes=["l3f_acol"])
        ta = a.sb("l3f_ta", [128, D], F32)
        tb = a.sb("l3f_tb", [128, D], F32)
        t16 = a.sb("l3f_t16", [128, D], BF16)
        for rc in range(DC):
            p.op("sync", lambda s, rc=rc: s.dma_start(out=wtile[:], in_=w_d[rc * 128:(rc + 1) * 128, :]),
                 reads=[w_d.name], writes=["l3f_wtile"], inc=16)
            p.op("vector", lambda v: v.tensor_scalar(ta[:], wtile[:], dcol[:, 0:1], None, ALU.is_gt),
                 reads=["l3f_wtile", "l3f_dcol"], writes=["l3f_ta"])
            p.op("vector", lambda v: v.tensor_scalar(tb[:], wtile[:], dcol[:, 1:2], None, ALU.is_lt),
                 reads=["l3f_wtile", "l3f_dcol"], writes=["l3f_tb"])
            p.op("vector", lambda v: v.tensor_sub(ta[:], ta[:], tb[:]),
                 reads=["l3f_ta", "l3f_tb"], writes=["l3f_ta"])
            p.op("vector", lambda v: v.tensor_scalar(t16[:], ta[:], acol[:], None, ALU.mult),
                 reads=["l3f_ta", "l3f_acol"], writes=["l3f_t16"])
            p.op("sync", lambda s, rc=rc: s.dma_start(out=wt_d[rc * 128:(rc + 1) * 128, :], in_=t16[:]),
                 reads=["l3f_t16"], writes=["l3fwt"], inc=16)
        wts = a.sb("l3f_wts", [128, DC * D], BF16)
        for kc in range(DC):
            p.op("sync", lambda s, kc=kc: s.dma_start(out=wts[:, kc * D:(kc + 1) * D], in_=wt_d[0:D, kc * 128:(kc + 1) * 128], transpose=True),
                 reads=["l3fwt"], writes=["l3f_wts"], inc=16)
        bias = load_row(p, a, io["L3_ffn_bias"], "l3f_bias")
        xf = [a.sb(f"l3f_xf{kc}", [128, T], BF16) for kc in range(DC)]
        xfk = [f"l3f_xf{kc}" for kc in range(DC)]
        fm_loads(p, xf, xfk, xn16)
        proj_tok(p, xf, xfk, wts, "l3f_wts", D, bias, "l3f_bias", ones_row, "ones_row",
                 psb[0:4], psk[0:4], accum_epilogue_tok(p, scratch, x_res))



def make_common(nc, ctx, p, a):
    scratch = {}
    psb = [a.ps(f"psb{i}", [128, 512], F32) for i in range(8)]
    psk = [f"psb{i}" for i in range(8)]
    scratch["psb"] = psb
    scratch["psk"] = psk
    ident_d = nc.inline_tensor(np.eye(128, dtype=np.float32), "identity128")
    ident = a.sb("ident_sb", [128, 128], F32)
    p.op("sync", lambda s: s.dma_start(out=ident[:], in_=ident_d[:]),
         reads=[], writes=["ident_sb"], inc=16)
    scratch["ident_sb"] = ident
    scratch["bounce_ring"] = [a.sb(f"bnc{i}", [128, 512], F32) for i in range(2)]
    scratch["bounce_ctr"] = [0]
    p.const_pool = a.sb("cstpool", [128, 32], F32)
    ones_row = a.sb("ones_row", [1, 512], F32)
    p.op("vector", lambda v: v.memset(ones_row[:], 1.0), writes=["ones_row"])
    ones_col16 = a.sb("ones_col16", [128, 1], BF16)
    p.op("vector", lambda v: v.memset(ones_col16[:], 1.0), writes=["ones_col16"])
    scratch["ones_col16"] = ones_col16
    ones_row16 = a.sb("ones_row16", [1, 512], BF16)
    p.op("vector", lambda v: v.memset(ones_row16[:], 1.0), writes=["ones_row16"])
    scratch["ones_row16"] = ones_row16
    zeros_row = a.sb("zeros_row", [1, 1024], BF16)
    p.op("vector", lambda v: v.memset(zeros_row[:], 0.0), writes=["zeros_row"])
    scratch["ones_row"] = ones_row
    scratch["zeros_row"] = zeros_row
    scratch["xn16"] = nc.dram_tensor("xn16_scratch", [T, D], BF16, kind="Internal")
    scratch["dram_f32_DT"] = nc.dram_tensor("dram_f32_DT", [D, T], F32, kind="Internal")
    scratch["dram_bf16_DT"] = nc.dram_tensor("dram_bf16_DT", [D, T], BF16, kind="Internal")
    return scratch


def declare_weight_inputs(nc, names_shapes):
    io = {}
    for name, shape in names_shapes:
        shp = list(shape) if len(shape) > 1 else [1, shape[0]] if False else list(shape)
        io[name] = nc.dram_tensor(name, list(shape), F32, kind="ExternalInput")
    return io


L1_WEIGHTS = [
    ("L1_attn_W_in", (D, D)), ("L1_attn_b_in", (D,)),
    ("L1_attn_W_gx", (D, D)), ("L1_attn_b_gx", (D,)),
    ("L1_attn_W_ga", (D, D)), ("L1_attn_b_ga", (D,)),
    ("L1_attn_a_param", (D,)), ("L1_attn_W_out", (D, D)), ("L1_attn_b_out", (D,)),
    ("L1_ffn_CB", (D, D)), ("L1_ffn_SCB", (D,)), ("L1_ffn_bias", (D,)),
]


def build_program_L0():
    nc = bass.Bass("TRN2", target_bir_lowering=False)
    ctx = ExitStack()
    x_in = nc.dram_tensor("x_in", [T, D], F32, kind="ExternalInput")
    x_out = nc.dram_tensor("x_out", [T, D], F32, kind="ExternalOutput")
    io = declare_weight_inputs(nc, L0_WEIGHTS)
    x_res = nc.dram_tensor("x_res", [T, D], F32, kind="Internal")

    p = Prog(nc, ctx)
    a = Arena(nc, ctx, p)
    scratch = make_common(nc, ctx, p, a)

    p.op("sync", lambda s: s.dma_start(out=x_res[:], in_=x_in[:]),
         reads=[x_in.name], writes=[x_res.name], inc=16)
    p.barrier()
    dbg = {}
    if L0_DEBUG:
        dbg["q"] = nc.dram_tensor("dbg_q", [T, D], BF16, kind="ExternalOutput")
        dbg["k"] = nc.dram_tensor("dbg_k", [T, 256], BF16, kind="ExternalOutput")
        dbg["ctx"] = nc.dram_tensor("dbg_ctx", [D, T], BF16, kind="ExternalOutput")
    if L0_PART in (0, 99):
        build_L0_attn(p, a, io, x_res, scratch)
    if L0_PART in (1, 99):
        build_L0_ffn(p, a, io, x_res, scratch)
    if L0_DEBUG:
        p.op("sync", lambda s: s.dma_start(out=dbg["q"][:], in_=nc.m and scratch["_l0q16d"][:]),
             reads=["l0q16d"], writes=["dbgq"], inc=16)
        p.op("sync", lambda s: s.dma_start(out=dbg["k"][:], in_=scratch["_l0k16d"][:]),
             reads=["l0k16d"], writes=["dbgk"], inc=16)
        p.op("sync", lambda s: s.dma_start(out=dbg["ctx"][:], in_=scratch["dram_bf16_DT"][:]),
             reads=["l0ctxd"], writes=["dbgc"], inc=16)
    p.op("sync", lambda s: s.dma_start(out=x_out[:], in_=x_res[:]),
         reads=[x_res.name], writes=[x_out.name], inc=16)
    p.finalize()
    ctx.close()
    return nc


L0_PART = 99
L0_DEBUG = False
L2_PART = 99
L3_PART = 99
L3_DEBUG = False


def build_program_L3():
    nc = bass.Bass("TRN2", target_bir_lowering=False)
    ctx = ExitStack()
    x_in = nc.dram_tensor("x_in", [T, D], F32, kind="ExternalInput")
    x_out = nc.dram_tensor("x_out", [T, D], F32, kind="ExternalOutput")
    io = declare_weight_inputs(nc, L3_WEIGHTS)
    x_res = nc.dram_tensor("x_res", [T, D], F32, kind="Internal")

    p = Prog(nc, ctx)
    a = Arena(nc, ctx, p)
    scratch = make_common(nc, ctx, p, a)

    p.op("sync", lambda s: s.dma_start(out=x_res[:], in_=x_in[:]),
         reads=[x_in.name], writes=[x_res.name], inc=16)
    p.barrier()
    dbg = {}
    if L3_DEBUG:
        dbg["wt"] = nc.dram_tensor("dbg_wt", [D, D], BF16, kind="ExternalOutput")
        dbg["dd"] = nc.dram_tensor("dbg_dd", [1, 8], F32, kind="ExternalOutput")
    if L3_PART in (0, 99):
        build_L3_attn(p, a, io, x_res, scratch)
    if L3_PART in (1, 99):
        build_L3_ffn(p, a, io, x_res, scratch)
    if L3_DEBUG:
        p.op("sync", lambda s: s.dma_start(out=dbg["wt"][:], in_=scratch["_l3f_wt"][:]),
             reads=["l3fwt"], writes=["dbgwt"], inc=16)
        p.op("sync", lambda s: s.dma_start(out=dbg["dd"][:], in_=scratch["_l3f_dbg"][0:1, 0:8]),
             reads=["l3f_dbgrow"], writes=["dbgdd"], inc=16)
    p.op("sync", lambda s: s.dma_start(out=x_out[:], in_=x_res[:]),
         reads=[x_res.name], writes=[x_out.name], inc=16)
    p.finalize()
    ctx.close()
    return nc


def build_program_L2():
    nc = bass.Bass("TRN2", target_bir_lowering=False)
    ctx = ExitStack()
    x_in = nc.dram_tensor("x_in", [T, D], F32, kind="ExternalInput")
    x_out = nc.dram_tensor("x_out", [T, D], F32, kind="ExternalOutput")
    io = declare_weight_inputs(nc, L2_WEIGHTS)
    x_res = nc.dram_tensor("x_res", [T, D], F32, kind="Internal")

    p = Prog(nc, ctx)
    a = Arena(nc, ctx, p)
    scratch = make_common(nc, ctx, p, a)

    p.op("sync", lambda s: s.dma_start(out=x_res[:], in_=x_in[:]),
         reads=[x_in.name], writes=[x_res.name], inc=16)
    p.barrier()
    if L2_PART in (0, 99):
        build_L2_attn(p, a, io, x_res, scratch)
    if L2_PART in (1, 99):
        build_L2_ffn(p, a, io, x_res, scratch)
    p.op("sync", lambda s: s.dma_start(out=x_out[:], in_=x_res[:]),
         reads=[x_res.name], writes=[x_out.name], inc=16)
    p.finalize()
    ctx.close()
    return nc



def build_program_L1():
    nc = bass.Bass("TRN2", target_bir_lowering=False)
    ctx = ExitStack()
    x_in = nc.dram_tensor("x_in", [T, D], F32, kind="ExternalInput")
    x_out = nc.dram_tensor("x_out", [T, D], F32, kind="ExternalOutput")
    io = declare_weight_inputs(nc, L1_WEIGHTS)
    x_res = nc.dram_tensor("x_res", [T, D], F32, kind="Internal")

    p = Prog(nc, ctx)
    a = Arena(nc, ctx, p)
    scratch = make_common(nc, ctx, p, a)

    p.op("sync", lambda s: s.dma_start(out=x_res[:], in_=x_in[:]),
         reads=[x_in.name], writes=[x_res.name], inc=16)
    p.barrier()
    build_L1_attn(p, a, io, x_res, scratch)
    if L1_STAGE >= 5:
        build_L1_ffn(p, a, io, x_res, scratch)
    p.op("sync", lambda s: s.dma_start(out=x_out[:], in_=x_res[:]),
         reads=[x_res.name], writes=[x_out.name], inc=16)
    p.finalize()
    ctx.close()
    return nc


ALL_WEIGHTS = None


def build_program_full():
    nc = bass.Bass("TRN2", target_bir_lowering=False)
    ctx = ExitStack()
    x_in = nc.dram_tensor("x_in", [T, D], F32, kind="ExternalInput")
    x_out = nc.dram_tensor("x_out", [T, D], F32, kind="ExternalOutput")
    io = declare_weight_inputs(nc, ALL_WEIGHTS)
    x_res = nc.dram_tensor("x_res", [T, D], F32, kind="Internal")

    p = Prog(nc, ctx)
    a = Arena(nc, ctx, p)
    scratch = make_common(nc, ctx, p, a)

    p.op("sync", lambda s: s.dma_start(out=x_res[:], in_=x_in[:]),
         reads=[x_in.name], writes=[x_res.name], inc=16)
    p.barrier()
    build_L0_attn(p, a, io, x_res, scratch)
    build_L0_ffn(p, a, io, x_res, scratch)
    build_L1_attn(p, a, io, x_res, scratch)
    build_L1_ffn(p, a, io, x_res, scratch)
    build_L2_attn(p, a, io, x_res, scratch)
    build_L2_ffn(p, a, io, x_res, scratch)
    build_L3_attn(p, a, io, x_res, scratch)
    build_L3_ffn(p, a, io, x_res, scratch)
    p.op("sync", lambda s: s.dma_start(out=x_out[:], in_=x_res[:]),
         reads=[x_res.name], writes=[x_out.name], inc=16)
    p.finalize()
    ctx.close()
    return nc


def _run_stage(builder, wlist, shards, inputs):
    nc = builder()
    in_maps = []
    for c in range(N_CORES):
        m = {"x_in": shards[c]}
        for name, _ in wlist:
            m[name] = inputs[name]
        in_maps.append(m)
    res = run_bass_kernel_spmd(nc, in_maps, core_ids=list(range(N_CORES)))
    return [np.ascontiguousarray(res.results[c]["x_out"]) for c in range(N_CORES)]


ALL_WEIGHTS = L0_WEIGHTS + L1_WEIGHTS + L2_WEIGHTS + L3_WEIGHTS
FUSED = True


def kernel(**inputs):
    global L0_PART, L2_PART, L3_PART, L1_STAGE, L1_SUB, L0_DEBUG, L3_DEBUG
    L1_STAGE = L1_SUB = 99
    L0_DEBUG = L3_DEBUG = False
    x = np.ascontiguousarray(np.asarray(inputs["L0_attn_hidden_states"], dtype=np.float32))
    B = x.shape[0]
    shards = [np.ascontiguousarray(x[c]) for c in range(N_CORES)]

    if FUSED:
        shards = _run_stage(build_program_full, ALL_WEIGHTS, shards, inputs)
        return np.stack(shards).reshape(B, T, D)

    stages = [
        ("L0_PART", 0, build_program_L0, L0_WEIGHTS),
        ("L0_PART", 1, build_program_L0, L0_WEIGHTS),
        (None, None, build_program_L1, L1_WEIGHTS),
        ("L2_PART", 0, build_program_L2, L2_WEIGHTS),
        ("L2_PART", 1, build_program_L2, L2_WEIGHTS),
        ("L3_PART", 0, build_program_L3, L3_WEIGHTS),
        ("L3_PART", 1, build_program_L3, L3_WEIGHTS),
    ]
    for var, val, builder, wlist in stages:
        if var is not None:
            globals()[var] = val
        shards = _run_stage(builder, wlist, shards, inputs)
    return np.stack(shards).reshape(B, T, D)
